# version 44
# baseline (speedup 1.0000x reference)
import sys
sys.path.insert(0, '/opt/trn_rl_repo')
import numpy as np
import concourse.bass as bass
import concourse.mybir as mybir
from concourse.bass import IndirectOffsetOnAxis

F32 = mybir.dt.float32
BF16 = mybir.dt.bfloat16
I32 = mybir.dt.int32
U32 = mybir.dt.uint32
AF = mybir.ActivationFunctionType
ALU = mybir.AluOpType
AX = mybir.AxisListType

T = 2048
D = 1024
NC8 = 8
NTT = 16
INTER = 4096
NIC = 32
NEXP = 16384
TBLK = 256
NTB = T // TBLK

ENGS = ("sync", "vector", "scalar", "tensor", "gpsimd")


class Sem:
    def __init__(self, h):
        self.h = h
        self.n = 0

    def bump(self, k):
        self.n += k
        return self.n


class P:

    AUTO = {"vector": "dve", "scalar": "act", "gpsimd": "poolc"}

    def __init__(self, nc, sems):
        self.nc = nc
        self.S = sems
        self.q = {e: [] for e in ENGS}

    def on(self, eng, fn):
        self.q[eng].append(fn)

    def wait(self, eng, sem, v=None):
        if isinstance(sem, str):
            sem = self.S[sem]
        if v is None:
            v = sem.n
        if v > 0:
            self.on(eng, lambda e, s=sem.h, vv=v: e.wait_ge(s, vv))

    def dma(self, eng, out, in_, sem):
        if isinstance(sem, str):
            sem = self.S[sem]
        self.on(eng, lambda e, o=out, i=in_, s=sem.h: e.dma_start(out=o, in_=i).then_inc(s, 16))
        return sem.bump(16)

    def op(self, eng, fn):
        a = self.AUTO.get(eng)
        if a is None:
            self.on(eng, lambda e, f=fn: f(e))
            return None
        s = self.S[a]
        if s.n > 0:
            self.on(eng, lambda e, h=s.h, v=s.n: e.wait_ge(h, v))
        self.on(eng, lambda e, f=fn, h=s.h: f(e).then_inc(h, 1))
        return s.bump(1)

    def dma_indirect_gather(self, out, table, idx_ap, sem):
        self.on("gpsimd", lambda e, o=out, t=table, i=idx_ap, s=sem.h: e.indirect_dma_start(
            out=o, out_offset=None, in_=t,
            in_offset=IndirectOffsetOnAxis(ap=i, axis=0)).then_inc(s, 16))
        return sem.bump(16)

    def pe(self, fn, inc=False):
        if inc:
            s = self.S["pe"]
            self.on("tensor", lambda e, f=fn, h=s.h: f(e).then_inc(h, 1))
            return s.bump(1)
        self.on("tensor", lambda e, f=fn: f(e))
        return None

    def replay(self, block):
        for name in ENGS:
            fns = self.q[name]

            def body(eng, fns=fns):
                for f in fns:
                    f(eng)

            getattr(block, name)(body)


def ts(i, n=128):
    return slice(i * n, (i + 1) * n)


def build(debug=(), upto="full"):
    nc = bass.Bass("TRN2", target_bir_lowering=False)

    d_x = nc.dram_tensor("x_fm", [D, T], F32, kind="ExternalInput").ap()
    d_vec = nc.dram_tensor("vecs", [128, 64], F32, kind="ExternalInput").ap()
    d_wr = nc.dram_tensor("wrB", [NC8, D, 128], F32, kind="ExternalInput").ap()
    d_wk = nc.dram_tensor("wkB", [NC8, D, 128], F32, kind="ExternalInput").ap()
    d_wv = nc.dram_tensor("wvB", [NC8, D, 128], F32, kind="ExternalInput").ap()
    d_wo = nc.dram_tensor("woB", [NC8, D, 128], F32, kind="ExternalInput").ap()
    d_wq = nc.dram_tensor("wqB", [4, D, 128], F32, kind="ExternalInput").ap()
    d_wg = nc.dram_tensor("wgB", [NIC, D, 128], F32, kind="ExternalInput").ap()
    d_wu = nc.dram_tensor("wuB", [NIC, D, 128], F32, kind="ExternalInput").ap()
    d_wd = nc.dram_tensor("wdT", [INTER, D], F32, kind="ExternalInput").ap()
    d_keys = nc.dram_tensor("keysP", [2, 4, 64, 128], F32, kind="ExternalInput").ap()
    d_down = nc.dram_tensor("downE", [NEXP, D], F32, kind="ExternalInput").ap()
    d_up = nc.dram_tensor("upE", [NEXP, D], F32, kind="ExternalInput").ap()
    d_out = nc.dram_tensor("out_fm", [D, T], F32, kind="ExternalOutput").ap()

    d_tabb = nc.dram_tensor("tabb", [NEXP, 2048], BF16, kind="Internal").ap()
    d_wrb = nc.dram_tensor("wrb", [NC8, D, 128], BF16, kind="Internal").ap()
    d_wkb = nc.dram_tensor("wkb", [NC8, D, 128], BF16, kind="Internal").ap()
    d_wvb = nc.dram_tensor("wvb", [NC8, D, 128], BF16, kind="Internal").ap()
    d_wob = nc.dram_tensor("wob", [NC8, D, 128], BF16, kind="Internal").ap()
    d_wqb = nc.dram_tensor("wqb", [4, D, 128], BF16, kind="Internal").ap()
    d_wgb = nc.dram_tensor("wgb", [NIC, D, 128], BF16, kind="Internal").ap()
    d_wub = nc.dram_tensor("wub", [NIC, D, 128], BF16, kind="Internal").ap()
    d_wdb = nc.dram_tensor("wdb", [INTER, D], BF16, kind="Internal").ap()
    d_h1 = nc.dram_tensor("h1_scr", [NTB, INTER, TBLK], BF16, kind="Internal").ap()
    d_exp = nc.dram_tensor("exp_scr", [NTT, D, 128], BF16, kind="Internal").ap()
    skind = "ExternalOutput" if "s" in debug else "Internal"
    d_s = nc.dram_tensor("dbg_s" if skind == "ExternalOutput" else "s_scr",
                         [D, T], F32, kind=skind).ap()

    dbg = {}

    def dbg_out(name, shape, dt=F32):
        if name in debug:
            dbg[name] = nc.dram_tensor("dbg_" + name, shape, dt, kind="ExternalOutput").ap()
            return dbg[name]
        return None

    guards = []

    def alloc(name, shape, dt):
        g = nc.sbuf_tensor(name, shape, dt)
        t = g.__enter__()
        guards.append(g)
        return t

    c_identf = alloc("c_identf", [128, 128], F32)
    c_identb = alloc("c_identb", [128, 128], BF16)
    c_iota64 = alloc("c_iota64", [128, 64], F32)
    c_vec = alloc("c_vec", [128, 64], F32)
    stat = alloc("stat", [128, 8], F32)
    xn_tm_t = alloc("xn_tm_t", [128, 1024], BF16)
    xn_fm = alloc("xn_fm", [128, NC8 * (T + 1)], BF16)
    big0 = alloc("big0", [128, 16384], F32)
    big1 = alloc("big1", [128, 24576], F32)
    wblk = alloc("wblk", [128, 2048], F32)

    sems = {}
    for s in ("lds", "ldv", "pool", "poolc", "pe", "dve", "act", "outs", "tabs", "ldk", "xcp", "cld", "smts", "aout", "wcg", "wcu", "wcd", "wdl", "cl0", "cl1",
              "la0", "la1", "la2", "la3", "la4", "la5", "la6", "la7",
              "wcr", "wck", "wcv", "wco", "wcq", "tc0", "tc1", "tcs", "tct", "zs0", "zs1", "hl0", "hl1", "wcg2", "wcu2"):
        g = nc.semaphore(s)
        h = g.__enter__()
        guards.append(g)
        sems[s] = Sem(h)

    p = P(nc, sems)
    S = sems

    def _truncate_finish():
        p.wait("sync", "dve")
        p.wait("sync", "act")
        p.wait("sync", "pe")
        p.wait("sync", "pool")
        p.dma("sync", d_out[0:128, 0:64], c_vec[:, 0:64], "outs")
        p.wait("sync", "outs")
        with nc.Block() as block:
            p.replay(block)
        for g in reversed(guards):
            g.__exit__(None, None, None)
        return nc

    b0b = big0.ap().bitcast(BF16)
    b1b = big1.ap().bitcast(BF16)

    wbb = wblk.ap().bitcast(BF16)
    wslot = [wbb[:, s * 1024:(s + 1) * 1024] for s in range(4)]

    x_c = [big0[:, c * T:(c + 1) * T] for c in range(NC8)]
    sr_c = [b0b[:, c * T:(c + 1) * T] for c in range(NC8)]
    rwkv_c = [b0b[:, 16384 + c * T:16384 + (c + 1) * T] for c in range(NC8)]
    z_t = big1[:, 0:2048]
    zv_t = big1[:, 2048:4096]
    z2_t = big1[:, 4096:6144]
    den_t = big1[:, 6144:8192]
    diff_b = b0b[:, 0:2048]
    diff_p = b0b[:, 2048:4096]
    q_c = [b0b[:, c * T:(c + 1) * T] for c in range(4)]

    xn_c = [xn_fm[:, c * (T + 1):(c + 1) * (T + 1)] for c in range(NC8)]
    xr_c = [b1b[:, c * T:(c + 1) * T] for c in range(NC8)]
    xk_c = [b1b[:, 16384 + c * T:16384 + (c + 1) * T] for c in range(NC8)]
    xv_c = [b1b[:, 32768 + c * T:32768 + (c + 1) * T] for c in range(NC8)]

    p.dma("sync", c_vec[:, 0:64], d_vec, "ldv")
    for c in range(NC8):
        p.dma("sync" if c % 2 == 0 else "gpsimd", x_c[c], d_x[ts(c), :], "la%d" % c)

    p.op("gpsimd", lambda e: e.iota(out=c_identf[:], pattern=[[1, 128]], base=0,
                                    channel_multiplier=-1,
                                    allow_small_or_imprecise_dtypes=True))
    p.op("gpsimd", lambda e: e.iota(out=c_iota64[:], pattern=[[1, 64]], base=0,
                                    channel_multiplier=0,
                                    allow_small_or_imprecise_dtypes=True))
    p.dma("gpsimd", d_wrb, d_wr, "wcr")
    p.dma("gpsimd", d_wkb, d_wk, "wck")
    p.dma("gpsimd", d_wvb, d_wv, "wcv")
    p.dma("gpsimd", d_wob, d_wo, "wco")
    p.dma("gpsimd", d_wqb, d_wq, "wcq")
    for ci8 in range(4):
        p.dma("gpsimd", d_tabb[ci8 * 1024:(ci8 + 1) * 1024, 0:1024],
              d_down[ci8 * 1024:(ci8 + 1) * 1024, :], "tabs")
        p.dma("gpsimd", d_tabb[ci8 * 1024:(ci8 + 1) * 1024, 1024:2048],
              d_up[ci8 * 1024:(ci8 + 1) * 1024, :], "tabs")
    p.wait("vector", "poolc")
    p.op("vector", lambda e: e.tensor_scalar(out=c_identb[:], in0=c_identf[:],
                                             scalar1=0.0, scalar2=None, op0=ALU.is_equal))
    p.op("vector", lambda e: e.tensor_scalar(out=c_identf[:], in0=c_identf[:],
                                             scalar1=0.0, scalar2=None, op0=ALU.is_equal))
    p.wait("scalar", "ldv", 16)
    p.op("scalar", lambda e: e.activation(out=c_vec[:, 24:32], in_=c_vec[:, 24:32], func=AF.Exp))
    p.op("scalar", lambda e: e.activation(out=c_vec[:, 24:32], in_=c_vec[:, 24:32],
                                          func=AF.Exp, scale=-1.0))
    p.wait("vector", "ldv", 16)
    p.op("vector", lambda e: e.tensor_scalar(out=c_vec[:, 40:48], in0=c_vec[:, 32:40],
                                             scalar1=0.0, scalar2=None, op0=ALU.max))
    for c in range(NC8):
        p.op("vector", lambda e, c=c: e.memset(xn_c[c][:, 0:1], 0.0))

    if upto == "P":
        return _truncate_finish()

    psg = []

    def palloc(name, shape, dt):
        g = nc.psum_tensor(name, shape, dt)
        t = g.__enter__()
        psg.append(g)
        return t

    def pfree():
        while psg:
            psg.pop().__exit__(None, None, None)

    ps_sum = palloc("ps_sum", [128, 2048], F32)
    ps_sq = palloc("ps_sq", [128, 2048], F32)

    d_mix = dbg_out("mix", [128, 3 * 16384], BF16)

    xsq = [big1[:, 0:2048], big1[:, 2048:4096]]
    rowS = big1[0:1, 4096:6144]
    rowQ = big1[0:1, 6144:8192]
    rowC = big1[0:1, 8192:10240]
    ones_col = big1[:, 10240:10241]
    ones_row = big1[0:1, 10241:10369]
    tmpA = big1[:, 4096:6144]
    tmpP = big1[:, 6144:8192]
    sbA = big1[:, 0:2048]
    sbC = big1[:, 2048:4096]

    p.op("vector", lambda e: e.memset(ones_col[:], 1.0))
    p.op("vector", lambda e: e.memset(ones_row[:], 1.0))
    v_ones = S["dve"].n
    p.wait("tensor", S["dve"], v_ones)
    sq_done = [0] * NC8
    pe_sq = [0] * NC8
    for c in range(NC8):
        if c >= 2:
            p.wait("scalar", S["pe"], pe_sq[c - 2])
        p.wait("scalar", S["la%d" % c], 16)
        p.wait("tensor", S["la%d" % c], 16)
        sq_done[c] = p.op("scalar", lambda e, c=c: e.activation(out=xsq[c % 2][:],
                                                                in_=x_c[c][:], func=AF.Square))
        for n in range(4):
            p.pe(lambda e, c=c, n=n: e.matmul(
                ps_sum[0:1, ts(n, 512)], ones_col[:], x_c[c][:, ts(n, 512)],
                start=(c == 0), stop=(c == NC8 - 1)), inc=(n == 3))
        p.wait("tensor", S["act"], sq_done[c])
        for n in range(4):
            p.pe(lambda e, c=c, n=n: e.matmul(
                ps_sq[0:1, ts(n, 512)], ones_col[:], xsq[c % 2][:, ts(n, 512)],
                start=(c == 0), stop=(c == NC8 - 1)), inc=(n == 3))
        pe_sq[c] = S["pe"].n
    v_pe_all = S["pe"].n
    p.wait("scalar", S["pe"], v_pe_all)
    p.op("scalar", lambda e: e.activation(out=rowS[:], in_=ps_sum[0:1, :], func=AF.Copy))
    v_eQ = p.op("scalar", lambda e: e.activation(out=rowQ[:], in_=ps_sq[0:1, :], func=AF.Copy))
    p.wait("vector", S["act"], v_eQ)
    p.op("vector", lambda e: e.tensor_scalar(out=rowS[:], in0=rowS[:], scalar1=1.0 / D,
                                             scalar2=None, op0=ALU.mult))
    p.op("vector", lambda e: e.tensor_scalar(out=rowQ[:], in0=rowQ[:], scalar1=1.0 / D,
                                             scalar2=None, op0=ALU.mult))
    p.op("vector", lambda e: e.tensor_tensor(out=rowC[:], in0=rowS[:], in1=rowS[:], op=ALU.mult))
    v_var = p.op("vector", lambda e: e.scalar_tensor_tensor(out=rowQ[:], in0=rowQ[:], scalar=1e-5,
                                                            in1=rowC[:], op0=ALU.add,
                                                            op1=ALU.subtract))
    p.wait("scalar", S["dve"], v_var)
    p.op("scalar", lambda e: e.activation(out=rowQ[:], in_=rowQ[:], func=AF.Ln))
    v_rstd = p.op("scalar", lambda e: e.activation(out=rowQ[:], in_=rowQ[:], func=AF.Exp,
                                                   scale=-0.5))
    p.wait("vector", S["act"], v_rstd)
    v_cr = p.op("vector", lambda e: e.scalar_tensor_tensor(out=rowC[:], in0=rowS[:], scalar=-1.0,
                                                           in1=rowQ[:], op0=ALU.mult,
                                                           op1=ALU.mult))
    p.wait("tensor", S["act"], v_rstd)
    p.wait("tensor", S["dve"], v_cr)
    for n in range(4):
        p.pe(lambda e, n=n: e.matmul(ps_sum[:, ts(n, 512)], ones_row[:],
                                     rowQ[:, ts(n, 512)], start=True, stop=True), inc=(n == 3))
    for n in range(4):
        p.pe(lambda e, n=n: e.matmul(ps_sq[:, ts(n, 512)], ones_row[:],
                                     rowC[:, ts(n, 512)], start=True, stop=True), inc=(n == 3))
    v_bc = S["pe"].n
    p.wait("scalar", S["pe"], v_bc)
    p.op("scalar", lambda e: e.activation(out=sbA[:], in_=ps_sum[:], func=AF.Copy))
    v_sb = p.op("scalar", lambda e: e.activation(out=sbC[:], in_=ps_sq[:], func=AF.Copy))
    p.wait("vector", S["act"], v_sb)
    p.wait("gpsimd", S["act"], v_sb)
    for c in range(NC8):
        eng = "gpsimd" if c % 2 == 1 else "vector"
        tm = tmpP if c % 2 == 1 else tmpA
        p.op(eng, lambda e, c=c, tm=tm: e.tensor_tensor(out=tm[:], in0=x_c[c][:],
                                                        in1=sbA[:], op=ALU.mult))
        p.op(eng, lambda e, c=c, tm=tm: e.tensor_tensor(out=xn_c[c][:, 1:T + 1],
                                                        in0=tm[:], in1=sbC[:], op=ALU.add))
    v_app_d = S["dve"].n
    v_app_p = S["poolc"].n

    if upto == "A":
        return _truncate_finish()

    p.wait("vector", "act")
    p.wait("vector", S["poolc"], v_app_p)
    p.wait("gpsimd", S["dve"], v_app_d)
    for c in range(NC8):
        cur = xn_c[c][:, 1:T + 1]
        prev = xn_c[c][:, 0:T]
        if c in (1, 3, 5):
            p.op("gpsimd", lambda e, cur=cur, prev=prev: e.tensor_tensor(
                out=diff_p[:], in0=cur, in1=prev, op=ALU.subtract))
            for (dst, col) in ((xr_c, 16), (xk_c, 0), (xv_c, 8)):
                muB = c_vec[:, col + c:col + c + 1].broadcast_to([128, T])
                p.op("gpsimd", lambda e, dst=dst, c=c, muB=muB: e.tensor_tensor(
                    out=dst[c][:], in0=diff_p[:], in1=muB, op=ALU.mult))
                p.op("gpsimd", lambda e, dst=dst, c=c, prev=prev: e.tensor_tensor(
                    out=dst[c][:], in0=dst[c][:], in1=prev, op=ALU.add))
        else:
            p.op("vector", lambda e, cur=cur, prev=prev: e.tensor_tensor(
                out=diff_b[:], in0=cur, in1=prev, op=ALU.subtract))
            for (dst, col) in ((xr_c, 16), (xk_c, 0), (xv_c, 8)):
                p.op("vector", lambda e, dst=dst, c=c, col=col, prev=prev: e.scalar_tensor_tensor(
                    out=dst[c][:], in0=diff_b[:], scalar=c_vec[:, col + c:col + c + 1], in1=prev,
                    op0=ALU.mult, op1=ALU.add))
    v_mixes = S["dve"].n
    v_mixp = S["poolc"].n

    if d_mix is not None:
        p.wait("sync", "dve")
        p.wait("sync", "poolc")
        for i, arr in enumerate((xr_c, xk_c, xv_c)):
            for c in range(NC8):
                p.dma("sync", d_mix[:, (i * 8 + c) * T:(i * 8 + c + 1) * T], arr[c][:], "outs")

    pfree()

    ps_r = palloc("ps_r", [128, 2048], F32)
    ps_v = palloc("ps_v", [128, 2048], F32)

    d_sr = dbg_out("sr", [128, 16384], BF16)
    d_rwkv = dbg_out("rwkv", [128, 16384], BF16)

    wsems = []
    for i in range(4):
        g = nc.semaphore(f"ws{i}")
        h = g.__enter__()
        guards.append(g)
        wsems.append(Sem(h))
    wbsems = []
    for i in range(4):
        g = nc.semaphore(f"wb{i}")
        h = g.__enter__()
        guards.append(g)
        wbsems.append(Sem(h))
    slot_pe_gate = [0, 0, 0, 0]

    def wload(slot, d_blocks, blk, csem):
        p.wait("sync", S["pe"], slot_pe_gate[slot])
        p.wait("sync", S[csem], 16)
        src_ap = d_blocks[blk].rearrange("(kc p) n -> p kc n", p=128)
        dst = wslot[slot].rearrange("p (kc n) -> p kc n", kc=NC8)
        return p.dma("sync", dst, src_ap, wbsems[slot])

    def proj_mm(ps, slot, rhs_c, nk=NC8):
        p.wait("tensor", wsems[slot])
        p.wait("tensor", wbsems[slot])
        for kc in range(nk):
            lhsT = wslot[slot][:, kc * 128:(kc + 1) * 128]
            for n in range(4):
                p.pe(lambda e, lhsT=lhsT, kc=kc, n=n, ps=ps, rhs=rhs_c[kc], nk=nk: e.matmul(
                    ps[:, ts(n, 512)], lhsT, rhs[:, ts(n, 512)],
                    start=(kc == 0), stop=(kc == nk - 1)),
                    inc=(kc == nk - 1 and n == 3))
        slot_pe_gate[slot] = S["pe"].n
        return S["pe"].n

    conv_state = [8]

    def emit_conv(n=1):
        NCCH = 16
        for _ in range(n):
            i = conv_state[0]
            if i >= 2 * NCCH:
                return
            conv_state[0] += 1
            j = i
            half, ci = j % 2, j // 2
            sl = slice(ci * NEXP // NCCH, (ci + 1) * NEXP // NCCH)
            if half == 0:
                p.dma("gpsimd", d_tabb[sl, 0:1024], d_down[sl, :], "tabs")
            else:
                p.dma("gpsimd", d_tabb[sl, 1024:2048], d_up[sl, :], "tabs")

    p.wait("tensor", S["dve"], v_mixes)
    p.wait("tensor", S["poolc"], v_mixp)
    wload(0, d_wrb, 0, "wcr")
    wload(1, d_wrb, 1, "wcr")
    v_sig_b0 = [0, 0]
    for m in range(NC8):
        psm = ps_r if m % 2 == 0 else ps_v
        p.wait("tensor", S["act"], v_sig_b0[m % 2])
        v_pe = proj_mm(psm, m % 2, xr_c)
        if m + 2 < NC8:
            wload(m % 2, d_wrb, m + 2, "wcr")
        elif m == NC8 - 2:
            wload(0, d_wkb, 0, "wck")
        elif m == NC8 - 1:
            wload(1, d_wkb, 1, "wck")
        emit_conv()
        p.wait("scalar", S["pe"], v_pe)
        v_sig_b0[m % 2] = p.op("scalar", lambda e, m=m, psm=psm: e.activation(
            out=sr_c[m][:], in_=psm[:], func=AF.Sigmoid))

    if d_sr is not None:
        p.wait("sync", "act")
        for c in range(NC8):
            p.dma("sync", d_sr[:, c * T:(c + 1) * T], sr_c[c][:], "outs")

    wload(2, d_wvb, 0, "wcv")
    wload(3, d_wvb, 1, "wcv")
    p.wait("tensor", S["act"], v_sig_b0[1])
    v_relk, v_relv = [v_sig_b0[0]], [0]
    for m in range(NC8):
        p.wait("tensor", S["act"], v_relk[0])
        v_pek = proj_mm(ps_r, m % 2, xk_c)
        if m + 2 < NC8:
            wload(m % 2, d_wkb, m + 2, "wck")
        elif m == NC8 - 2:
            wload(0, d_wob, 0, "wco")
        elif m == NC8 - 1:
            wload(1, d_wob, 1, "wco")
        p.wait("tensor", S["dve"], v_relv[0])
        v_pev = proj_mm(ps_v, 2 + m % 2, xv_c)
        if m + 2 < NC8:
            wload(2 + m % 2, d_wvb, m + 2, "wcv")
        emit_conv()
        p.wait("vector", S["pe"], v_pek)
        p.op("vector", lambda e: e.reduce_max(stat[:, 0:1], ps_r[:], axis=AX.X))
        p.op("vector", lambda e, m=m: e.tensor_tensor(out=stat[:, 1:2], in0=stat[:, 0:1],
                                                      in1=c_vec[:, 40 + m:41 + m], op=ALU.add))
        p.op("vector", lambda e: e.tensor_scalar(out=stat[:, 2:3], in0=stat[:, 1:2],
                                                 scalar1=-1.0, scalar2=None, op0=ALU.mult))
        v = p.op("vector", lambda e, m=m: e.tensor_tensor(out=stat[:, 3:4], in0=c_vec[:, 32 + m:33 + m],
                                                          in1=stat[:, 1:2], op=ALU.subtract))
        p.wait("scalar", S["dve"], v)
        p.op("scalar", lambda e: e.activation(out=z_t[:], in_=ps_r[:], func=AF.Exp, bias=stat[:, 2:3]))
        v_z2 = p.op("scalar", lambda e: e.activation(out=z2_t[:], in_=ps_r[:], func=AF.Exp, bias=stat[:, 3:4]))
        p.wait("vector", S["act"], v_z2)
        p.wait("vector", S["pe"], v_pev)
        p.op("vector", lambda e: e.tensor_tensor(out=zv_t[:], in0=z_t[:], in1=ps_v[:], op=ALU.mult))
        p.op("vector", lambda e, m=m: e.tensor_tensor_scan(
            out=zv_t[:], data0=c_vec[:, 24 + m:25 + m].broadcast_to([128, T]),
            data1=zv_t[:], initial=0.0, op0=ALU.mult, op1=ALU.add))
        p.op("vector", lambda e, m=m: e.tensor_tensor_scan(
            out=z_t[:], data0=c_vec[:, 24 + m:25 + m].broadcast_to([128, T]),
            data1=z_t[:], initial=0.0, op0=ALU.mult, op1=ALU.add))
        p.op("vector", lambda e: e.tensor_copy(out=den_t[:, 0:1], in_=z2_t[:, 0:1]))
        v_den = p.op("vector", lambda e: e.tensor_tensor(out=den_t[:, 1:T], in0=z_t[:, 0:T - 1],
                                                         in1=z2_t[:, 1:T], op=ALU.add))
        v_euv = p.op("vector", lambda e: e.tensor_tensor(out=z2_t[:], in0=z2_t[:], in1=ps_v[:], op=ALU.mult))
        p.wait("scalar", S["dve"], v_den)
        p.op("scalar", lambda e: e.activation(out=den_t[:], in_=den_t[:], func=AF.Ln))
        v_rden = p.op("scalar", lambda e: e.activation(out=den_t[:], in_=den_t[:], func=AF.Exp, scale=-1.0))
        p.op("vector", lambda e: e.tensor_tensor(out=z2_t[:, 1:T], in0=zv_t[:, 0:T - 1],
                                                 in1=z2_t[:, 1:T], op=ALU.add))
        p.wait("vector", S["act"], v_rden)
        v_y = p.op("vector", lambda e: e.tensor_tensor(out=z2_t[:], in0=z2_t[:], in1=den_t[:], op=ALU.mult))
        p.op("vector", lambda e, m=m: e.tensor_tensor(out=rwkv_c[m][:], in0=sr_c[m][:],
                                                      in1=z2_t[:], op=ALU.mult))
        v_relk[0] = v_z2
        v_relv[0] = v_euv

    if d_rwkv is not None:
        p.wait("sync", "dve")
        for c in range(NC8):
            p.dma("sync", d_rwkv[:, c * T:(c + 1) * T], rwkv_c[c][:], "outs")

    p.wait("tensor", "dve")
    p.wait("tensor", "act")
    p.wait("scalar", "dve")
    v_evac_b2 = [0, 0]
    for m in range(NC8):
        psm = ps_r if m % 2 == 0 else ps_v
        p.wait("tensor", S["act"], v_evac_b2[m % 2])
        v_pe = proj_mm(psm, m % 2, rwkv_c)
        if m + 2 < NC8:
            wload(m % 2, d_wob, m + 2, "wco")
        elif m == NC8 - 2:
            wload(0, d_wqb, 0, "wcq")
        elif m == NC8 - 1:
            wload(1, d_wqb, 1, "wcq")
        emit_conv()
        zt = z_t if m % 2 == 0 else zv_t
        zsem = "zs0" if m % 2 == 0 else "zs1"
        p.wait("scalar", S["pe"], v_pe)
        p.wait("scalar", S[zsem])
        v = p.op("scalar", lambda e, zt=zt, psm=psm: e.activation(out=zt[:], in_=psm[:],
                                                                  func=AF.Copy))
        v_evac_b2[m % 2] = v
        p.wait("sync", S["act"], v)
        p.dma("sync", d_s[ts(m), :], zt[:], zsem)

    xn1_c = [xn_c[kc][:, 1:T + 1] for kc in range(NC8)]
    v_qe = [v_evac_b2[1], v_evac_b2[0]]
    for m in range(4):
        psm = ps_v if m % 2 == 0 else ps_r
        p.wait("tensor", S["act"], v_qe[m % 2])
        v_pe = proj_mm(psm, m % 2, xn1_c)
        if m + 2 < 4:
            wload(m % 2, d_wqb, m + 2, "wcq")
        emit_conv(2)
        p.wait("scalar", S["pe"], v_pe)
        v_qe[m % 2] = p.op("scalar", lambda e, m=m, psm=psm: e.activation(
            out=q_c[m][:], in_=psm[:], func=AF.Copy))
    p.wait("tensor", S["act"], v_qe[0])
    p.wait("tensor", S["act"], v_qe[1])
    d_q = dbg_out("q", [128, 8192], BF16)
    if d_q is not None:
        p.wait("sync", "act")
        for m in range(4):
            p.dma("sync", d_q[:, m * T:(m + 1) * T], q_c[m][:], "outs")


    if upto == "B":
        return _truncate_finish()

    pfree()

    ps_mlp = palloc("ps_mlp", [128, 2048], F32)
    ps_se = palloc("ps_se", [128, 1024], F32)
    psb2 = palloc("ps_b2", [128, 128], BF16)
    psb3 = palloc("ps_b3", [128, 128], BF16)

    d_dots = dbg_out("dots", [NTT, 128, 32])
    d_expid = dbg_out("expi", [NTT, 128, 32], I32)
    d_probsd = dbg_out("probs", [NTT, 128, 32])
    d_expfm = dbg_out("expfm", [NTT, 1024, 128], BF16)
    d_h1d = dbg_out("h1", [NTB, INTER, TBLK], BF16)

    sx4 = alloc("sx4", [128, 64], F32)
    ixu = alloc("ixu", [128, 64], U32)
    ixf = alloc("ixf", [128, 64], F32)
    itmp = alloc("itmp", [128, 128], I32)
    tops = alloc("tops", [128, 96], F32)
    sel = alloc("sel", [128, 64], F32)
    expi = [alloc(f"expi{i}", [128, 32], I32) for i in range(4)]
    probs2 = alloc("probs2", [128, 128], F32)
    smx = alloc("smx", [128, 16], F32)
    dots = alloc("dots", [128, 32], F32)
    wfin = alloc("wfin", [128, 32], F32)
    sim_s = big0[:, 12288:13312]
    sgb = b0b[:, 26624:28672]
    pdot = [b0b[:, 28672:29696], b0b[:, 29696:30720]]
    keys_t = b0b[:, 30720:31744]
    expstg = b0b[:, 31744:32768]
    diag_g = alloc("diag_g", [128, 512], BF16)
    xtv = xn_tm_t.ap().bitcast(F32)
    cand = xtv[:, 0:256]
    packB = xtv[:, 256:512]


    gslot = ([b0b[:, 8192 + s * 2048:8192 + (s + 1) * 2048] for s in range(8)] +
             [b1b[:, s * 2048:(s + 1) * 2048] for s in range(16)])
    NSLOT = 24
    h1tb = b0b[:, 8192:16384]
    exptb = [b1b[:, 32768 + i * 1024:32768 + (i + 1) * 1024] for i in range(2)]
    xmt = big1[:, 17408:17920]
    wd_b = b1b[:, 0:32768]
    xn_tm = b1b[:, 32768:49152]

    gsems = []
    for i in range(24):
        g = nc.semaphore(f"gs{i}")
        h = g.__enter__()
        guards.append(g)
        gsems.append(Sem(h))
    gslot_pe_gate = [0] * 24
    gslot_cnt = [0] * 24

    emit_conv(40)
    v_tab = S["tabs"].n
    p.wait("gpsimd", "pe")
    p.wait("gpsimd", "act")
    p.dma("gpsimd", keys_t[0:64, :].rearrange("p (g n) -> p g n", g=8),
          d_keys.rearrange("p h n k -> n (p h) k"), "ldk")
    p.dma("gpsimd", keys_t[64:128, :].rearrange("p (g n) -> p g n", g=8),
          d_keys.rearrange("p h n k -> n (p h) k"), "ldk")
    v_keys = S["ldk"].n
    p.dma("gpsimd", d_wgb[0:8], d_wg[0:8], "wcg")
    p.dma("gpsimd", d_wub[0:8], d_wu[0:8], "wcu")
    p.dma("gpsimd", d_wgb[8:NIC], d_wg[8:NIC], "wcg2")
    p.dma("gpsimd", d_wub[8:NIC], d_wu[8:NIC], "wcu2")
    p.dma("gpsimd", d_wdb, d_wd, "wcd")

    p.wait("tensor", "dve")
    p.wait("tensor", "act")
    evac_done = [0, 0]
    for i in range(NTT * NC8):
        tt, c = i // NC8, i % NC8
        pb = psb2 if i % 2 == 0 else psb3
        p.wait("tensor", S["act"], evac_done[i % 2])
        p.pe(lambda e, tt=tt, c=c, pb=pb: e.transpose(
            pb[:], xn_c[c][:, 1 + tt * 128:1 + (tt + 1) * 128], c_identb[:]), inc=True)
        p.wait("scalar", "pe")
        evac_done[i % 2] = p.op("scalar", lambda e, tt=tt, c=c, pb=pb: e.activation(
            out=xn_tm[:, tt * 1024 + c * 128:tt * 1024 + (c + 1) * 128],
            in_=pb[:], func=AF.Copy))
    v_xntm = S["act"].n

    if upto == "CPRE":
        return _truncate_finish()

    p.wait("gpsimd", "pe")
    p.wait("gpsimd", "dve")
    p.wait("gpsimd", "outs")
    p.wait("gpsimd", S["zs0"])
    p.wait("gpsimd", S["zs1"])
    p.wait("tensor", S["ldk"], v_keys)
    g_h1 = nc.semaphore("h1s")
    h1sem = Sem(g_h1.__enter__())
    guards.append(g_h1)

    v_expi_tile = [0, 0, 0, 0]
    pdot_act = [0, 0]
    exp_evac_act = 0
    v_up_prev = 0
    ic_counter = 0
    wd_first_use = True
    silu_act_group = [0] * 16

    topk_pend = [None]
    exp_pend = [None]

    def emit_topk(tt, defer=False):
        if topk_pend[0] is not None:
            for _ in topk_pend[0]:
                pass
            topk_pend[0] = None
        par = tt % 4
        p.wait("tensor", "act")
        for ph in range(8):
            pp, hh = ph // 4, ph % 4
            row = pp * 256 + hh * 64
            qt = q_c[row // 128]
            base = row % 128
            v = p.pe(lambda e, qt=qt, base=base, tt=tt, ph=ph: e.matmul(
                ps_se[:, ts(ph)], qt[base:base + 64, ts(tt)],
                keys_t[base:base + 64, ts(ph)], start=True, stop=True), inc=True)
            p.wait("tensor", S["pe"], v)
        v_pe = S["pe"].n
        p.wait("scalar", S["pe"], v_pe)
        v_evac = p.op("scalar", lambda e: e.activation(out=sim_s[:], in_=ps_se[:], func=AF.Copy))
        gen = topk_stage(tt, v_evac)
        if defer:
            topk_pend[0] = gen
        else:
            for _ in gen:
                pass

    def topk_stage(tt, v_evac):
        par = tt % 4
        p.wait("vector", S["act"], v_evac)
        for ph in range(8):
            p.op("vector", lambda e, ph=ph: e.max(out=sx4[:, ph * 8:(ph + 1) * 8],
                                                  in_=sim_s[:, ts(ph)]))
            p.op("vector", lambda e, ph=ph: e.max_index(out=ixu[:, ph * 8:(ph + 1) * 8],
                                                        in_max=sx4[:, ph * 8:(ph + 1) * 8],
                                                        in_values=sim_s[:, ts(ph)]))
            if ph % 2 == 1:
                yield
        p.op("vector", lambda e: e.tensor_copy(out=ixf[:], in_=ixu[:]))
        yield
        for h in range(4):
            sxv = sx4[:, h * 8:(h + 1) * 8].unsqueeze(2).broadcast_to([128, 8, 8])
            syv = sx4[:, 32 + h * 8:32 + (h + 1) * 8].unsqueeze(1).broadcast_to([128, 8, 8])
            ch = cand[:, h * 64:(h + 1) * 64].rearrange("p (i j) -> p i j", i=8)
            p.op("vector", lambda e, ch=ch, sxv=sxv, syv=syv: e.tensor_tensor(
                out=ch, in0=sxv, in1=syv, op=ALU.add))
        yield
        p.op("vector", lambda e: e.tensor_scalar(out=cand[:], in0=cand[:], scalar1=31.9,
                                                 scalar2=-31.9, op0=ALU.min, op1=ALU.max))
        for h in range(4):
            p.op("vector", lambda e, h=h: e.max(out=tops[:, h * 8:(h + 1) * 8],
                                                in_=cand[:, h * 64:(h + 1) * 64]))
        yield
        p.op("vector", lambda e: e.tensor_scalar(out=cand[:], in0=cand[:], scalar1=32.0,
                                                 scalar2=2048.0, op0=ALU.add, op1=ALU.mult))
        for hh2 in range(2):
            p.op("vector", lambda e, hh2=hh2: e.tensor_copy(out=itmp[:],
                                                            in_=cand[:, hh2 * 128:(hh2 + 1) * 128]))
            p.op("vector", lambda e, hh2=hh2: e.tensor_copy(out=cand[:, hh2 * 128:(hh2 + 1) * 128],
                                                            in_=itmp[:]))
        yield
        ixv = ixf[:, 0:32].unsqueeze(2).broadcast_to([128, 32, 8])
        cB = cand.rearrange("p (hi j) -> p hi j", j=8)
        pB = packB.rearrange("p (hi j) -> p hi j", j=8)
        p.op("vector", lambda e, cB=cB, pB=pB, ixv=ixv: e.scalar_tensor_tensor(
            out=pB, in0=cB, scalar=128.0, in1=ixv, op0=ALU.mult, op1=ALU.add))
        yield
        for h in range(4):
            iyv = ixf[:, 32 + h * 8:32 + (h + 1) * 8].unsqueeze(1).broadcast_to([128, 8, 8])
            ch = cand[:, h * 64:(h + 1) * 64].rearrange("p (i j) -> p i j", i=8)
            p.op("vector", lambda e, ch=ch, iyv=iyv: e.scalar_tensor_tensor(
                out=ch, in0=ch, scalar=128.0, in1=iyv, op0=ALU.mult, op1=ALU.add))
        yield
        for h in range(4):
            p.op("vector", lambda e, h=h: e.max(out=tops[:, 32 + h * 8:32 + (h + 1) * 8],
                                                in_=packB[:, h * 64:(h + 1) * 64]))
            p.op("vector", lambda e, h=h: e.max(out=tops[:, 64 + h * 8:64 + (h + 1) * 8],
                                                in_=cand[:, h * 64:(h + 1) * 64]))
        yield
        p.op("vector", lambda e: e.tensor_copy(out=itmp[:, 0:64], in_=tops[:, 32:96]))
        p.op("vector", lambda e: e.tensor_scalar(out=itmp[:, 64:128], in0=itmp[:, 0:64],
                                                 scalar1=127, scalar2=None, op0=ALU.bitwise_and))
        p.op("vector", lambda e: e.tensor_copy(out=sel[:], in_=itmp[:, 64:128]))
        p.op("vector", lambda e: e.scalar_tensor_tensor(out=sel[:, 0:32],
                                                        in0=sel[:, 0:32], scalar=128.0,
                                                        in1=sel[:, 32:64], op0=ALU.mult, op1=ALU.add))
        yield
        for s8 in range(NSLOT):
            p.wait("vector", gsems[s8], 16 * gslot_cnt[s8])
        vi = p.op("vector", lambda e: e.tensor_copy(out=expi[par][:], in_=sel[:, 0:32]))
        v_expi_tile[par] = vi
        yield
        t8 = tops[:, 0:32].rearrange("p (h r) -> p h r", h=4)
        p.op("vector", lambda e: e.tensor_reduce(out=smx[:, 0:4].rearrange("p (h r) -> p h r", r=1),
                                                 in_=t8, axis=AX.X, op=ALU.max))
        mxv = smx[:, 0:4].rearrange("p (h r) -> p h r", r=1).broadcast_to([128, 4, 8])
        v = p.op("vector", lambda e: e.tensor_tensor(out=t8, in0=t8, in1=mxv, op=ALU.subtract))
        yield
        p.wait("scalar", S["dve"], v)
        v = p.op("scalar", lambda e: e.activation(out=tops[:, 0:32], in_=tops[:, 0:32], func=AF.Exp))
        p.wait("vector", S["act"], v)
        p.op("vector", lambda e: e.tensor_reduce(out=smx[:, 4:8].rearrange("p (h r) -> p h r", r=1),
                                                 in_=t8, axis=AX.X, op=ALU.add))
        p.op("vector", lambda e: e.reciprocal(out=smx[:, 8:12], in_=smx[:, 4:8]))
        rsv = smx[:, 8:12].rearrange("p (h r) -> p h r", r=1).broadcast_to([128, 4, 8])
        pr = probs2[:, par * 32:(par + 1) * 32].rearrange("p (h r) -> p h r", h=4)
        p.op("vector", lambda e, pr=pr, rsv=rsv, t8=t8: e.tensor_tensor(out=pr, in0=t8, in1=rsv, op=ALU.mult))
        if d_expid is not None:
            p.wait("sync", "dve")
            p.dma("sync", d_expid[tt], expi[par][:], "outs")
            p.dma("sync", d_probsd[tt], probs2[:, par * 32:(par + 1) * 32], "outs")

    def emit_experts(tt):
        nonlocal exp_evac_act, v_up_prev
        par = tt % 4
        xnt = xn_tm[:, tt * 1024:(tt + 1) * 1024]
        GS = 2
        NG = 32 // GS
        pend = None

        def finish_group(g4, slots, v_silu):
            nonlocal v_up_prev
            sl4 = slice(g4 * GS, (g4 + 1) * GS)
            p.wait("vector", S["act"], v_silu)
            p.op("vector", lambda e, sl4=sl4, par=par: e.tensor_tensor(
                out=wfin[:, sl4], in0=wfin[:, sl4],
                in1=probs2[:, par * 32 + g4 * GS:par * 32 + (g4 + 1) * GS], op=ALU.mult))
            iv = c_identb.ap().unsqueeze(1).broadcast_to([128, GS, 128])
            wv = wfin[:, sl4].unsqueeze(2).broadcast_to([128, GS, 128])
            dgbase = (g4 % 2) * GS
            dv = diag_g[:, dgbase * 128:(dgbase + GS) * 128].rearrange("p (j c) -> p j c", j=GS)
            p.wait("vector", S["pe"], v_up_prev)
            v_diag = p.op("vector", lambda e, iv=iv, wv=wv, dv=dv: e.tensor_tensor(
                out=dv, in0=iv, in1=wv, op=ALU.mult))
            p.wait("tensor", S["dve"], v_diag)
            for i4, (j, slot) in enumerate(slots):
                for n in range(2):
                    p.pe(lambda e, i4=i4, slot=slot, n=n, j=j, dgbase=dgbase: e.matmul(
                        ps_se[:, ts(n, 512)], diag_g[:, (dgbase + i4) * 128:(dgbase + i4 + 1) * 128],
                        gslot[slot][:, 1024 + n * 512:1024 + (n + 1) * 512],
                        start=(j == 0), stop=(j == 31)),
                        inc=(n == 1))
                gslot_pe_gate[slot] = S["pe"].n
            v_up_prev = S["pe"].n

        fg_count = [0]
        for g4 in range(NG):
            slots = []
            for j4 in range(GS):
                j = g4 * GS + j4
                slot = (tt * 32 + j) % NSLOT
                slots.append((j, slot))
                p.wait("gpsimd", S["pe"], gslot_pe_gate[slot])
                p.wait("gpsimd", S["dve"], v_expi_tile[par])
                p.wait("gpsimd", S["tabs"], v_tab)
                p.dma_indirect_gather(gslot[slot], d_tabb, expi[par][:, j:j + 1], gsems[slot])
                gslot_cnt[slot] += 1
            v_sttd = 0
            for j4x, (j, slot) in enumerate(slots):
                p.wait("vector", gsems[slot], 16 * gslot_cnt[slot])
                p.wait("vector", S["act"], v_xntm)
                if j4x == 0 and g4 % 3 == 0:
                    p.wait("vector", S["act"], silu_act_group[g4])
                    v_sttd = p.op("vector", lambda e, slot=slot, xnt=xnt, j=j: e.scalar_tensor_tensor(
                        out=gslot[slot][:, 0:1024], in0=gslot[slot][:, 0:1024], scalar=1.0,
                        in1=xnt, op0=ALU.mult, op1=ALU.mult, accum_out=dots[:, j:j + 1]))
                    continue
                pb2 = pdot[j % 2]
                p.wait("vector", S["act"], pdot_act[j % 2])
                vP = p.op("vector", lambda e, slot=slot, xnt=xnt, pb2=pb2: e.tensor_tensor(
                    out=pb2, in0=gslot[slot][:, 0:1024], in1=xnt, op=ALU.mult))
                p.wait("scalar", S["dve"], vP)
                pdot_act[j % 2] = p.op("scalar", lambda e, pb2=pb2, j=j: e.activation(
                    out=pb2, in_=pb2, func=AF.Copy, accum_out=dots[:, j:j + 1]))
            sl4 = slice(g4 * GS, (g4 + 1) * GS)
            if v_sttd:
                p.wait("scalar", S["dve"], v_sttd)
            v_silu = p.op("scalar", lambda e, sl4=sl4: e.activation(out=wfin[:, sl4],
                                                                    in_=dots[:, sl4], func=AF.Silu))
            silu_act_group[g4] = v_silu
            if pend is not None:
                finish_group(*pend)
            pend = (g4, slots, v_silu)
            gu_quantum()
            for _ in range(2):
                if topk_pend[0] is not None:
                    if next(topk_pend[0], "END") == "END":
                        topk_pend[0] = None
            if exp_pend[0] is not None:
                if next(exp_pend[0], "END") == "END":
                    exp_pend[0] = None
        finish_group(*pend)
        if d_dots is not None:
            p.wait("sync", "dve")
            p.dma("sync", d_dots[tt], dots[:], "outs")
        if exp_pend[0] is not None:
            for _ in exp_pend[0]:
                pass
            exp_pend[0] = None
        p.wait("scalar", S["pe"], v_up_prev)
        p.wait("scalar", "outs")
        v = p.op("scalar", lambda e: e.activation(out=expstg[:], in_=ps_se[:], func=AF.Copy))
        exp_pend[0] = evac_gen(tt, v)

    def evac_gen(tt, v_evac):
        tdone = [v_evac, v_evac]
        for c in range(NC8):
            pb = psb2 if c % 2 == 0 else psb3
            p.wait("tensor", S["act"], tdone[c % 2])
            p.pe(lambda e, c=c, pb=pb: e.transpose(pb[:], expstg[:, ts(c)], c_identb[:]), inc=True)
            p.wait("scalar", "pe")
            tdone[c % 2] = p.op("scalar", lambda e, c=c, pb=pb: e.activation(
                out=expstg[:, ts(c)], in_=pb[:], func=AF.Copy))
            if c % 2 == 1:
                yield
        p.wait("sync", "act")
        p.dma("sync", d_exp[tt].rearrange("(c p) n -> p c n", p=128),
              expstg.rearrange("p (c n) -> p c n", c=NC8), "outs")
        if d_expfm is not None:
            p.dma("sync", d_expfm[tt].rearrange("(c p) n -> p c n", p=128),
                  expstg.rearrange("p (c n) -> p c n", c=NC8), "outs")

    def wload_b(slot, d_blocks, blk):
        p.wait("sync", S["pe"], slot_pe_gate[slot])
        p.wait("sync", S["wcg"], 16)
        p.wait("sync", S["wcu"], 16)
        if blk >= 8:
            p.wait("sync", S["wcg2"], 16)
            p.wait("sync", S["wcu2"], 16)
        src_ap = d_blocks[blk].rearrange("(kc p) n -> p kc n", p=128)
        dst = wslot[slot].rearrange("p (kc n) -> p kc n", kc=NC8)
        return p.dma("sync", dst, src_ap, wbsems[slot])

    NQ = 4
    gu2 = {"ic": 0, "half": 0, "q": 0, "v_pe_g": 0, "v_pe_u": 0,
           "v_silu": 0, "v_h1": 0}

    def gu_end_u(ic):
        p.wait("vector", S["pe"], gu2["v_pe_u"])
        p.wait("vector", S["act"], gu2["v_silu"])
        v_h1 = p.op("vector", lambda e: e.tensor_tensor(out=sgb[:], in0=sgb[:],
                                                        in1=ps_mlp[:], op=ALU.mult))
        gu2["v_h1"] = v_h1
        p.wait("sync", S["dve"], v_h1)
        for tb8 in range(NTB):
            p.dma("sync", d_h1[tb8][ic * 128:(ic + 1) * 128, :],
                  sgb[:, tb8 * TBLK:(tb8 + 1) * TBLK], h1sem)
        if d_h1d is not None:
            for tb8 in range(NTB):
                p.dma("sync", d_h1d[tb8][ic * 128:(ic + 1) * 128, :],
                      sgb[:, tb8 * TBLK:(tb8 + 1) * TBLK], "outs")

    def gu_quantum():
        ic, half, q = gu2["ic"], gu2["half"], gu2["q"]
        if ic >= NIC:
            return
        slot = ic % 2 if half == 0 else 2 + ic % 2
        if q == 0:
            if half == 0:
                if ic > 0:
                    gu_end_u(ic - 1)
                p.wait("tensor", wsems[slot])
                p.wait("tensor", wbsems[slot])
                if ic > 0:
                    p.wait("tensor", S["dve"], gu2["v_h1"])
            else:
                p.wait("scalar", S["pe"], gu2["v_pe_g"])
                p.wait("scalar", S["dve"], gu2["v_h1"])
                p.wait("scalar", h1sem)
                gu2["v_silu"] = p.op("scalar", lambda e: e.activation(
                    out=sgb[:], in_=ps_mlp[:], func=AF.Silu))
                p.wait("tensor", wsems[slot])
                p.wait("tensor", wbsems[slot])
                p.wait("tensor", S["act"], gu2["v_silu"])
        for kc in (2 * q, 2 * q + 1):
            lhsT = wslot[slot][:, kc * 128:(kc + 1) * 128]
            for n in range(4):
                p.pe(lambda e, lhsT=lhsT, kc=kc, n=n: e.matmul(
                    ps_mlp[:, ts(n, 512)], lhsT, xn1_c[kc][:, ts(n, 512)],
                    start=(kc == 0), stop=(kc == NC8 - 1)),
                    inc=(kc == 2 * q + 1 and n == 3))
        if q == NQ - 1:
            slot_pe_gate[slot] = S["pe"].n
            if half == 0:
                gu2["v_pe_g"] = S["pe"].n
                gu2["half"] = 1
            else:
                gu2["v_pe_u"] = S["pe"].n
                gu2["half"] = 0
                gu2["ic"] = ic + 1
            gu2["q"] = 0
        else:
            gu2["q"] = q + 1

    wload_b(0, d_wgb, 0)
    wload_b(1, d_wgb, 1)
    wload_b(2, d_wub, 0)
    wload_b(3, d_wub, 1)
    emit_topk(0)
    emit_topk(1, defer=True)
    for tb in range(NTB):
        ic0 = 4 * tb
        if tb > 0:
            wload_b(ic0 % 2, d_wgb, ic0)
            wload_b(2 + ic0 % 2, d_wub, ic0)
            wload_b((ic0 + 1) % 2, d_wgb, ic0 + 1)
            wload_b(2 + (ic0 + 1) % 2, d_wub, ic0 + 1)
        emit_experts(2 * tb)
        if 2 * tb + 2 < NTT:
            emit_topk(2 * tb + 2, defer=True)
        wload_b((ic0 + 2) % 2, d_wgb, ic0 + 2)
        wload_b(2 + (ic0 + 2) % 2, d_wub, ic0 + 2)
        wload_b((ic0 + 3) % 2, d_wgb, ic0 + 3)
        wload_b(2 + (ic0 + 3) % 2, d_wub, ic0 + 3)
        emit_experts(2 * tb + 1)
        if 2 * tb + 3 < NTT:
            emit_topk(2 * tb + 3, defer=True)
    while gu2["ic"] < NIC:
        gu_quantum()
    gu_end_u(NIC - 1)
    if exp_pend[0] is not None:
        for _ in exp_pend[0]:
            pass
        exp_pend[0] = None
    if topk_pend[0] is not None:
        for _ in topk_pend[0]:
            pass
        topk_pend[0] = None

    if upto == "CI":
        return _truncate_finish()

    p.wait("sync", "pe")
    p.wait("sync", S["wcd"], 16)
    p.dma("sync", wd_b.rearrange("p (ic m) -> p ic m", ic=NIC),
          d_wdb.rearrange("(ic p) m -> p ic m", p=128), "wdl")
    p.wait("tensor", S["pool"])
    p.wait("tensor", S["wdl"], 16)
    p.wait("sync", "act")
    p.wait("sync", "dve")
    p.wait("sync", "pe")
    p.wait("sync", h1sem)
    p.wait("sync", "outs")
    p.wait("sync", S["zs0"])
    p.wait("sync", S["zs1"])
    pfree()
    ps_m2 = [palloc("ps_m2a", [128, 2048], F32), palloc("ps_m2b", [128, 2048], F32)]
    p.wait("gpsimd", h1sem)
    p.wait("gpsimd", "pe")
    p.wait("gpsimd", "dve")
    xnf32 = xn_fm.ap().bitcast(F32)
    mrgT = big1[:, 17920:19968]
    stb = big1[:, 19968:22016]
    xtb = big1[:, 22016:24064]
    h1tb2 = [h1tb, b0b[:, 0:8192]]
    exptb2 = [exptb, [b1b[:, 34816:35840], b1b[:, 48128:49152]]]
    stb2 = [stb, xnf32[:, 0:2048]]
    xtb2 = [xtb, xnf32[:, 2048:4096]]
    mrgT2 = [mrgT, xnf32[:, 4096:6144]]
    v_pe_tb = [0] * NTB
    v_p1_tb = [0] * NTB
    v_t0_tb = [0] * NTB
    v_ld_tb = [0] * NTB

    v_hl_tb = [0] * NTB

    def cii_loads(tb):
        par = tb % 2
        csem = "cl0" if par == 0 else "cl1"
        hsem = "hl0" if par == 0 else "hl1"
        if tb >= 2:
            p.wait("sync", S["pe"], v_pe_tb[tb - 2])
            p.wait("sync", S["dve"], v_p1_tb[tb - 2])
            p.wait("gpsimd", S["pe"], v_pe_tb[tb - 2])
        p.dma("gpsimd", h1tb2[par].rearrange("p (q n) -> p q n", q=NIC),
              d_h1[tb].rearrange("(q p) n -> p q n", p=128), hsem)
        v_hl_tb[tb] = S[hsem].n
        for i in range(2):
            p.dma("sync", exptb2[par][i].rearrange("p (c n) -> p c n", c=NC8),
                  d_exp[2 * tb + i].rearrange("(c p) n -> p c n", p=128), csem)
        p.dma("sync", stb2[par].rearrange("p (m n) -> p m n", m=NC8),
              d_s[:, tb * TBLK:(tb + 1) * TBLK].rearrange("(m p) n -> p m n", p=128), csem)
        p.dma("sync", xtb2[par].rearrange("p (m n) -> p m n", m=NC8),
              d_x[:, tb * TBLK:(tb + 1) * TBLK].rearrange("(m p) n -> p m n", p=128), csem)
        v_ld_tb[tb] = S[csem].n

    cii_loads(0)
    cii_loads(1)
    for tb in range(NTB):
        par = tb % 2
        h1c, expc, stc, xtc, mrgc = (h1tb2[par], exptb2[par], stb2[par],
                                     xtb2[par], mrgT2[par])
        p.wait("tensor", S["cl0" if par == 0 else "cl1"], v_ld_tb[tb])
        p.wait("tensor", S["hl0" if par == 0 else "hl1"], v_hl_tb[tb])
        if tb > 1:
            p.wait("tensor", S["dve"], v_t0_tb[tb - 2])
        psm = ps_m2[par]
        for m in range(NC8):
            for ic in range(NIC):
                p.pe(lambda e, m=m, ic=ic, h1c=h1c, psm=psm: e.matmul(
                    psm[:, m * 256:(m + 1) * 256],
                    wd_b[:, ic * 1024 + m * 128:ic * 1024 + (m + 1) * 128],
                    h1c[:, ic * 256:(ic + 1) * 256],
                    start=(ic == 0), stop=(ic == NIC - 1)), inc=(ic == NIC - 1 and m == NC8 - 1))
        v_pe_tb[tb] = S["pe"].n
        p.wait("vector", S["pe"], v_pe_tb[tb])
        p.wait("vector", S["cl0" if par == 0 else "cl1"], v_ld_tb[tb])
        p.wait("vector", S["outs"])
        mview = mrgc.rearrange("p (m i n) -> p m i n", m=NC8, i=2)
        for i in range(2):
            p.op("vector", lambda e, i=i, mview=mview, expc=expc, psm=psm: e.tensor_tensor(
                out=mview[:, :, i, :],
                in0=psm.ap().rearrange("p (m i n) -> p m i n", m=NC8, i=2)[:, :, i, :],
                in1=expc[i].rearrange("p (c n) -> p c n", c=NC8), op=ALU.add))
        v_t0_tb[tb] = S["dve"].n
        p.wait("scalar", S["dve"], v_t0_tb[tb])
        v_sil = p.op("scalar", lambda e, mrgc=mrgc: e.activation(out=mrgc[:], in_=mrgc[:],
                                                                 func=AF.Silu))
        p.wait("vector", S["act"], v_sil)
        p.op("vector", lambda e, mrgc=mrgc, stc=stc: e.tensor_tensor(
            out=mrgc[:], in0=mrgc[:], in1=stc[:], op=ALU.mult))
        v_p1_tb[tb] = p.op("vector", lambda e, mrgc=mrgc, xtc=xtc: e.tensor_tensor(
            out=mrgc[:], in0=mrgc[:], in1=xtc[:], op=ALU.add))
        p.wait("sync", S["dve"], v_p1_tb[tb])
        p.dma("sync", d_out[:, tb * TBLK:(tb + 1) * TBLK].rearrange("(m p) n -> p m n", p=128),
              mrgc.rearrange("p (m n) -> p m n", m=NC8), "outs")
        if tb + 2 < NTB:
            cii_loads(tb + 2)

    p.wait("sync", "outs")
    p.wait("sync", "pool")
    p.wait("sync", "tabs")
    p.wait("sync", h1sem)
    pfree()
    with nc.Block() as block:
        p.replay(block)

    for g in reversed(guards):
        g.__exit__(None, None, None)
    return nc


def prep_inputs(x_b, tmk, tmv, tmr, td, tf, Wk, Wv, Wr, Wo, Wq):
    x_fm = np.ascontiguousarray(x_b.T).astype(np.float32)
    vecs = np.zeros((128, 64), np.float32)
    for i, v in enumerate((tmk, tmv, tmr, td, tf)):
        vecs[:, i * 8:(i + 1) * 8] = v.reshape(8, 128).T
    def blocks(W):
        WT = W.T.astype(np.float32)
        M = WT.shape[1]
        return np.ascontiguousarray(WT.reshape(WT.shape[0], M // 128, 128).transpose(1, 0, 2))

    return dict(
        x_fm=x_fm, vecs=vecs,
        wrB=blocks(Wr), wkB=blocks(Wk), wvB=blocks(Wv), woB=blocks(Wo), wqB=blocks(Wq),
    )


def prep_inputs_c(keys, down, up, Wg, Wu, Wd):
    def blocks(W):
        WT = W.T.astype(np.float32)
        M = WT.shape[1]
        return np.ascontiguousarray(WT.reshape(WT.shape[0], M // 128, 128).transpose(1, 0, 2))

    return dict(
        keysP=np.ascontiguousarray(np.transpose(keys, (2, 0, 3, 1))).astype(np.float32),
        downE=np.ascontiguousarray(down).astype(np.float32),
        upE=np.ascontiguousarray(up).astype(np.float32),
        wgB=blocks(Wg), wuB=blocks(Wu),
        wdT=np.ascontiguousarray(Wd.T).astype(np.float32),
    )


_NC_CACHE = None


def kernel(**inputs):
    global _NC_CACHE
    from concourse.bass_utils import run_bass_kernel_spmd

    x = np.asarray(inputs["L0_attn_x"])
    shared = prep_inputs_shared(
        np.asarray(inputs["L0_attn_time_mix_k"]),
        np.asarray(inputs["L0_attn_time_mix_v"]),
        np.asarray(inputs["L0_attn_time_mix_r"]),
        np.asarray(inputs["L0_attn_time_decay"]),
        np.asarray(inputs["L0_attn_time_first"]),
        np.asarray(inputs["L0_attn_Wk"]), np.asarray(inputs["L0_attn_Wv"]),
        np.asarray(inputs["L0_attn_Wr"]), np.asarray(inputs["L0_attn_Wo"]),
        np.asarray(inputs["L0_ffn_Wq"]), np.asarray(inputs["L0_ffn_keys"]),
        np.asarray(inputs["L0_ffn_down_embed"]), np.asarray(inputs["L0_ffn_up_embed"]),
        np.asarray(inputs["L0_ffn_Wg"]), np.asarray(inputs["L0_ffn_Wu"]),
        np.asarray(inputs["L0_ffn_Wd"]))

    in_maps = []
    for b in range(8):
        m = dict(shared)
        m["x_fm"] = np.ascontiguousarray(x[b].T).astype(np.float32)
        in_maps.append(m)

    if _NC_CACHE is None:
        _NC_CACHE = build()
    res = run_bass_kernel_spmd(_NC_CACHE, in_maps, core_ids=list(range(8)))
    out = np.stack([np.asarray(res.results[b]["out_fm"]).T for b in range(8)], 0)
    return np.ascontiguousarray(out).astype(np.float32)


def prep_inputs_shared(tmk, tmv, tmr, td, tf, Wk, Wv, Wr, Wo, Wq, keys, down, up, Wg, Wu, Wd):
    vecs = np.zeros((128, 64), np.float32)
    for i, v in enumerate((tmk, tmv, tmr, td, tf)):
        vecs[:, i * 8:(i + 1) * 8] = v.reshape(8, 128).T

    def blocks(W):
        WT = W.T.astype(np.float32)
        M = WT.shape[1]
        return np.ascontiguousarray(WT.reshape(WT.shape[0], M // 128, 128).transpose(1, 0, 2))

    return dict(
        vecs=vecs,
        wrB=blocks(Wr), wkB=blocks(Wk), wvB=blocks(Wv), woB=blocks(Wo), wqB=blocks(Wq),
        wgB=blocks(Wg), wuB=blocks(Wu),
        wdT=np.ascontiguousarray(Wd.T).astype(np.float32),
        keysP=np.ascontiguousarray(np.transpose(keys, (2, 0, 3, 1))).astype(np.float32),
        downE=np.ascontiguousarray(down).astype(np.float32),
        upE=np.ascontiguousarray(up).astype(np.float32),
    )


# version 45
# speedup vs baseline: 1.0621x; 1.0621x over previous
import sys
sys.path.insert(0, '/opt/trn_rl_repo')
import numpy as np
import concourse.bass as bass
import concourse.mybir as mybir
from concourse.bass import IndirectOffsetOnAxis

F32 = mybir.dt.float32
BF16 = mybir.dt.bfloat16
I32 = mybir.dt.int32
U32 = mybir.dt.uint32
AF = mybir.ActivationFunctionType
ALU = mybir.AluOpType
AX = mybir.AxisListType

T = 2048
D = 1024
NC8 = 8
NTT = 16
INTER = 4096
NIC = 32
NEXP = 16384
TBLK = 256
NTB = T // TBLK

ENGS = ("sync", "vector", "scalar", "tensor", "gpsimd")


class Sem:
    def __init__(self, h):
        self.h = h
        self.n = 0

    def bump(self, k):
        self.n += k
        return self.n


class P:

    AUTO = {"vector": "dve", "scalar": "act", "gpsimd": "poolc"}

    def __init__(self, nc, sems):
        self.nc = nc
        self.S = sems
        self.q = {e: [] for e in ENGS}

    def on(self, eng, fn):
        self.q[eng].append(fn)

    def wait(self, eng, sem, v=None):
        if isinstance(sem, str):
            sem = self.S[sem]
        if v is None:
            v = sem.n
        if v > 0:
            self.on(eng, lambda e, s=sem.h, vv=v: e.wait_ge(s, vv))

    def dma(self, eng, out, in_, sem):
        if isinstance(sem, str):
            sem = self.S[sem]
        self.on(eng, lambda e, o=out, i=in_, s=sem.h: e.dma_start(out=o, in_=i).then_inc(s, 16))
        return sem.bump(16)

    def opn(self, eng, fn):
        a = self.AUTO.get(eng)
        s = self.S[a]
        self.on(eng, lambda e, f=fn, h=s.h: f(e).then_inc(h, 1))
        return s.bump(1)

    def op(self, eng, fn):
        a = self.AUTO.get(eng)
        if a is None:
            self.on(eng, lambda e, f=fn: f(e))
            return None
        s = self.S[a]
        if s.n > 0:
            self.on(eng, lambda e, h=s.h, v=s.n: e.wait_ge(h, v))
        self.on(eng, lambda e, f=fn, h=s.h: f(e).then_inc(h, 1))
        return s.bump(1)

    def dma_indirect_gather(self, out, table, idx_ap, sem):
        self.on("gpsimd", lambda e, o=out, t=table, i=idx_ap, s=sem.h: e.indirect_dma_start(
            out=o, out_offset=None, in_=t,
            in_offset=IndirectOffsetOnAxis(ap=i, axis=0)).then_inc(s, 16))
        return sem.bump(16)

    def pe(self, fn, inc=False):
        if inc:
            s = self.S["pe"]
            self.on("tensor", lambda e, f=fn, h=s.h: f(e).then_inc(h, 1))
            return s.bump(1)
        self.on("tensor", lambda e, f=fn: f(e))
        return None

    def replay(self, block):
        for name in ENGS:
            fns = self.q[name]

            def body(eng, fns=fns):
                for f in fns:
                    f(eng)

            getattr(block, name)(body)


def ts(i, n=128):
    return slice(i * n, (i + 1) * n)


def build(debug=(), upto="full"):
    nc = bass.Bass("TRN2", target_bir_lowering=False)

    d_x = nc.dram_tensor("x_fm", [D, T], F32, kind="ExternalInput").ap()
    d_vec = nc.dram_tensor("vecs", [128, 64], F32, kind="ExternalInput").ap()
    d_wr = nc.dram_tensor("wrB", [NC8, D, 128], F32, kind="ExternalInput").ap()
    d_wk = nc.dram_tensor("wkB", [NC8, D, 128], F32, kind="ExternalInput").ap()
    d_wv = nc.dram_tensor("wvB", [NC8, D, 128], F32, kind="ExternalInput").ap()
    d_wo = nc.dram_tensor("woB", [NC8, D, 128], F32, kind="ExternalInput").ap()
    d_wq = nc.dram_tensor("wqB", [4, D, 128], F32, kind="ExternalInput").ap()
    d_wg = nc.dram_tensor("wgB", [NIC, D, 128], F32, kind="ExternalInput").ap()
    d_wu = nc.dram_tensor("wuB", [NIC, D, 128], F32, kind="ExternalInput").ap()
    d_wd = nc.dram_tensor("wdT", [INTER, D], F32, kind="ExternalInput").ap()
    d_keys = nc.dram_tensor("keysP", [2, 4, 64, 128], F32, kind="ExternalInput").ap()
    d_down = nc.dram_tensor("downE", [NEXP, D], F32, kind="ExternalInput").ap()
    d_up = nc.dram_tensor("upE", [NEXP, D], F32, kind="ExternalInput").ap()
    d_out = nc.dram_tensor("out_fm", [D, T], F32, kind="ExternalOutput").ap()

    d_tabb = nc.dram_tensor("tabb", [NEXP, 2048], BF16, kind="Internal").ap()
    d_wrb = nc.dram_tensor("wrb", [NC8, D, 128], BF16, kind="Internal").ap()
    d_wkb = nc.dram_tensor("wkb", [NC8, D, 128], BF16, kind="Internal").ap()
    d_wvb = nc.dram_tensor("wvb", [NC8, D, 128], BF16, kind="Internal").ap()
    d_wob = nc.dram_tensor("wob", [NC8, D, 128], BF16, kind="Internal").ap()
    d_wqb = nc.dram_tensor("wqb", [4, D, 128], BF16, kind="Internal").ap()
    d_wgb = nc.dram_tensor("wgb", [NIC, D, 128], BF16, kind="Internal").ap()
    d_wub = nc.dram_tensor("wub", [NIC, D, 128], BF16, kind="Internal").ap()
    d_wdb = nc.dram_tensor("wdb", [INTER, D], BF16, kind="Internal").ap()
    d_h1 = nc.dram_tensor("h1_scr", [NTB, INTER, TBLK], BF16, kind="Internal").ap()
    d_exp = nc.dram_tensor("exp_scr", [NTT, D, 128], BF16, kind="Internal").ap()
    skind = "ExternalOutput" if "s" in debug else "Internal"
    d_s = nc.dram_tensor("dbg_s" if skind == "ExternalOutput" else "s_scr",
                         [D, T], F32, kind=skind).ap()

    dbg = {}

    def dbg_out(name, shape, dt=F32):
        if name in debug:
            dbg[name] = nc.dram_tensor("dbg_" + name, shape, dt, kind="ExternalOutput").ap()
            return dbg[name]
        return None

    guards = []

    def alloc(name, shape, dt):
        g = nc.sbuf_tensor(name, shape, dt)
        t = g.__enter__()
        guards.append(g)
        return t

    c_identf = alloc("c_identf", [128, 128], F32)
    c_identb = alloc("c_identb", [128, 128], BF16)
    c_iota64 = alloc("c_iota64", [128, 64], F32)
    c_vec = alloc("c_vec", [128, 64], F32)
    stat = alloc("stat", [128, 8], F32)
    xn_tm_t = alloc("xn_tm_t", [128, 1024], BF16)
    xn_fm = alloc("xn_fm", [128, NC8 * (T + 1)], BF16)
    big0 = alloc("big0", [128, 16384], F32)
    big1 = alloc("big1", [128, 24576], F32)
    wblk = alloc("wblk", [128, 2048], F32)

    sems = {}
    for s in ("lds", "ldv", "pool", "poolc", "pe", "dve", "act", "outs", "tabs", "ldk", "xcp", "cld", "smts", "aout", "wcg", "wcu", "wcd", "wdl", "cl0", "cl1",
              "la0", "la1", "la2", "la3", "la4", "la5", "la6", "la7",
              "wcr", "wck", "wcv", "wco", "wcq", "tc0", "tc1", "tcs", "tct", "zs0", "zs1", "hl0", "hl1", "wcg2", "wcu2"):
        g = nc.semaphore(s)
        h = g.__enter__()
        guards.append(g)
        sems[s] = Sem(h)

    p = P(nc, sems)
    S = sems

    def _truncate_finish():
        p.wait("sync", "dve")
        p.wait("sync", "act")
        p.wait("sync", "pe")
        p.wait("sync", "pool")
        p.dma("sync", d_out[0:128, 0:64], c_vec[:, 0:64], "outs")
        p.wait("sync", "outs")
        with nc.Block() as block:
            p.replay(block)
        for g in reversed(guards):
            g.__exit__(None, None, None)
        return nc

    b0b = big0.ap().bitcast(BF16)
    b1b = big1.ap().bitcast(BF16)

    wbb = wblk.ap().bitcast(BF16)
    wslot = [wbb[:, s * 1024:(s + 1) * 1024] for s in range(4)]

    x_c = [big0[:, c * T:(c + 1) * T] for c in range(NC8)]
    sr_c = [b0b[:, c * T:(c + 1) * T] for c in range(NC8)]
    rwkv_c = [b0b[:, 16384 + c * T:16384 + (c + 1) * T] for c in range(NC8)]
    z_t = big1[:, 0:2048]
    zv_t = big1[:, 2048:4096]
    z2_t = big1[:, 4096:6144]
    den_t = big1[:, 6144:8192]
    diff_b = b0b[:, 0:2048]
    diff_p = b0b[:, 2048:4096]
    q_c = [b0b[:, c * T:(c + 1) * T] for c in range(4)]

    xn_c = [xn_fm[:, c * (T + 1):(c + 1) * (T + 1)] for c in range(NC8)]
    xr_c = [b1b[:, c * T:(c + 1) * T] for c in range(NC8)]
    xk_c = [b1b[:, 16384 + c * T:16384 + (c + 1) * T] for c in range(NC8)]
    xv_c = [b1b[:, 32768 + c * T:32768 + (c + 1) * T] for c in range(NC8)]

    p.dma("sync", c_vec[:, 0:64], d_vec, "ldv")
    for c in range(NC8):
        p.dma("sync" if c % 2 == 0 else "gpsimd", x_c[c], d_x[ts(c), :], "la%d" % c)

    p.op("gpsimd", lambda e: e.iota(out=c_identf[:], pattern=[[1, 128]], base=0,
                                    channel_multiplier=-1,
                                    allow_small_or_imprecise_dtypes=True))
    p.op("gpsimd", lambda e: e.iota(out=c_iota64[:], pattern=[[1, 64]], base=0,
                                    channel_multiplier=0,
                                    allow_small_or_imprecise_dtypes=True))
    p.dma("gpsimd", d_wrb, d_wr, "wcr")
    p.dma("gpsimd", d_wkb, d_wk, "wck")
    p.dma("gpsimd", d_wvb, d_wv, "wcv")
    p.dma("gpsimd", d_wob, d_wo, "wco")
    p.dma("gpsimd", d_wqb, d_wq, "wcq")
    for ci8 in range(4):
        p.dma("gpsimd", d_tabb[ci8 * 1024:(ci8 + 1) * 1024, 0:1024],
              d_down[ci8 * 1024:(ci8 + 1) * 1024, :], "tabs")
        p.dma("gpsimd", d_tabb[ci8 * 1024:(ci8 + 1) * 1024, 1024:2048],
              d_up[ci8 * 1024:(ci8 + 1) * 1024, :], "tabs")
    p.wait("vector", "poolc")
    p.op("vector", lambda e: e.tensor_scalar(out=c_identb[:], in0=c_identf[:],
                                             scalar1=0.0, scalar2=None, op0=ALU.is_equal))
    p.op("vector", lambda e: e.tensor_scalar(out=c_identf[:], in0=c_identf[:],
                                             scalar1=0.0, scalar2=None, op0=ALU.is_equal))
    p.wait("scalar", "ldv", 16)
    p.op("scalar", lambda e: e.activation(out=c_vec[:, 24:32], in_=c_vec[:, 24:32], func=AF.Exp))
    p.op("scalar", lambda e: e.activation(out=c_vec[:, 24:32], in_=c_vec[:, 24:32],
                                          func=AF.Exp, scale=-1.0))
    p.wait("vector", "ldv", 16)
    p.op("vector", lambda e: e.tensor_scalar(out=c_vec[:, 40:48], in0=c_vec[:, 32:40],
                                             scalar1=0.0, scalar2=None, op0=ALU.max))
    for c in range(NC8):
        p.op("vector", lambda e, c=c: e.memset(xn_c[c][:, 0:1], 0.0))

    if upto == "P":
        return _truncate_finish()

    psg = []

    def palloc(name, shape, dt):
        g = nc.psum_tensor(name, shape, dt)
        t = g.__enter__()
        psg.append(g)
        return t

    def pfree():
        while psg:
            psg.pop().__exit__(None, None, None)

    ps_sum = palloc("ps_sum", [128, 2048], F32)
    ps_sq = palloc("ps_sq", [128, 2048], F32)

    d_mix = dbg_out("mix", [128, 3 * 16384], BF16)

    xsq = [big1[:, 0:2048], big1[:, 2048:4096]]
    rowS = big1[0:1, 4096:6144]
    rowQ = big1[0:1, 6144:8192]
    rowC = big1[0:1, 8192:10240]
    ones_col = big1[:, 10240:10241]
    ones_row = big1[0:1, 10241:10369]
    tmpA = big1[:, 4096:6144]
    tmpP = big1[:, 6144:8192]
    sbA = big1[:, 0:2048]
    sbC = big1[:, 2048:4096]

    p.op("vector", lambda e: e.memset(ones_col[:], 1.0))
    p.op("vector", lambda e: e.memset(ones_row[:], 1.0))
    v_ones = S["dve"].n
    p.wait("tensor", S["dve"], v_ones)
    sq_done = [0] * NC8
    pe_sq = [0] * NC8
    for c in range(NC8):
        if c >= 2:
            p.wait("scalar", S["pe"], pe_sq[c - 2])
        p.wait("scalar", S["la%d" % c], 16)
        p.wait("tensor", S["la%d" % c], 16)
        sq_done[c] = p.op("scalar", lambda e, c=c: e.activation(out=xsq[c % 2][:],
                                                                in_=x_c[c][:], func=AF.Square))
        for n in range(4):
            p.pe(lambda e, c=c, n=n: e.matmul(
                ps_sum[0:1, ts(n, 512)], ones_col[:], x_c[c][:, ts(n, 512)],
                start=(c == 0), stop=(c == NC8 - 1)), inc=(n == 3))
        p.wait("tensor", S["act"], sq_done[c])
        for n in range(4):
            p.pe(lambda e, c=c, n=n: e.matmul(
                ps_sq[0:1, ts(n, 512)], ones_col[:], xsq[c % 2][:, ts(n, 512)],
                start=(c == 0), stop=(c == NC8 - 1)), inc=(n == 3))
        pe_sq[c] = S["pe"].n
    v_pe_all = S["pe"].n
    p.wait("scalar", S["pe"], v_pe_all)
    p.op("scalar", lambda e: e.activation(out=rowS[:], in_=ps_sum[0:1, :], func=AF.Copy))
    v_eQ = p.op("scalar", lambda e: e.activation(out=rowQ[:], in_=ps_sq[0:1, :], func=AF.Copy))
    p.wait("vector", S["act"], v_eQ)
    p.op("vector", lambda e: e.tensor_scalar(out=rowS[:], in0=rowS[:], scalar1=1.0 / D,
                                             scalar2=None, op0=ALU.mult))
    p.op("vector", lambda e: e.tensor_scalar(out=rowQ[:], in0=rowQ[:], scalar1=1.0 / D,
                                             scalar2=None, op0=ALU.mult))
    p.op("vector", lambda e: e.tensor_tensor(out=rowC[:], in0=rowS[:], in1=rowS[:], op=ALU.mult))
    v_var = p.op("vector", lambda e: e.scalar_tensor_tensor(out=rowQ[:], in0=rowQ[:], scalar=1e-5,
                                                            in1=rowC[:], op0=ALU.add,
                                                            op1=ALU.subtract))
    p.wait("scalar", S["dve"], v_var)
    p.op("scalar", lambda e: e.activation(out=rowQ[:], in_=rowQ[:], func=AF.Ln))
    v_rstd = p.op("scalar", lambda e: e.activation(out=rowQ[:], in_=rowQ[:], func=AF.Exp,
                                                   scale=-0.5))
    p.wait("vector", S["act"], v_rstd)
    v_cr = p.op("vector", lambda e: e.scalar_tensor_tensor(out=rowC[:], in0=rowS[:], scalar=-1.0,
                                                           in1=rowQ[:], op0=ALU.mult,
                                                           op1=ALU.mult))
    p.wait("tensor", S["act"], v_rstd)
    p.wait("tensor", S["dve"], v_cr)
    for n in range(4):
        p.pe(lambda e, n=n: e.matmul(ps_sum[:, ts(n, 512)], ones_row[:],
                                     rowQ[:, ts(n, 512)], start=True, stop=True), inc=(n == 3))
    for n in range(4):
        p.pe(lambda e, n=n: e.matmul(ps_sq[:, ts(n, 512)], ones_row[:],
                                     rowC[:, ts(n, 512)], start=True, stop=True), inc=(n == 3))
    v_bc = S["pe"].n
    p.wait("scalar", S["pe"], v_bc)
    p.op("scalar", lambda e: e.activation(out=sbA[:], in_=ps_sum[:], func=AF.Copy))
    v_sb = p.op("scalar", lambda e: e.activation(out=sbC[:], in_=ps_sq[:], func=AF.Copy))
    p.wait("vector", S["act"], v_sb)
    p.wait("gpsimd", S["act"], v_sb)
    for c in range(NC8):
        eng = "gpsimd" if c % 2 == 1 else "vector"
        tm = tmpP if c % 2 == 1 else tmpA
        p.op(eng, lambda e, c=c, tm=tm: e.tensor_tensor(out=tm[:], in0=x_c[c][:],
                                                        in1=sbA[:], op=ALU.mult))
        p.op(eng, lambda e, c=c, tm=tm: e.tensor_tensor(out=xn_c[c][:, 1:T + 1],
                                                        in0=tm[:], in1=sbC[:], op=ALU.add))
    v_app_d = S["dve"].n
    v_app_p = S["poolc"].n

    if upto == "A":
        return _truncate_finish()

    p.wait("vector", "act")
    p.wait("vector", S["poolc"], v_app_p)
    p.wait("gpsimd", S["dve"], v_app_d)
    for c in range(NC8):
        cur = xn_c[c][:, 1:T + 1]
        prev = xn_c[c][:, 0:T]
        if c in (1, 3, 5):
            p.op("gpsimd", lambda e, cur=cur, prev=prev: e.tensor_tensor(
                out=diff_p[:], in0=cur, in1=prev, op=ALU.subtract))
            for (dst, col) in ((xr_c, 16), (xk_c, 0), (xv_c, 8)):
                muB = c_vec[:, col + c:col + c + 1].broadcast_to([128, T])
                p.op("gpsimd", lambda e, dst=dst, c=c, muB=muB: e.tensor_tensor(
                    out=dst[c][:], in0=diff_p[:], in1=muB, op=ALU.mult))
                p.op("gpsimd", lambda e, dst=dst, c=c, prev=prev: e.tensor_tensor(
                    out=dst[c][:], in0=dst[c][:], in1=prev, op=ALU.add))
        else:
            p.op("vector", lambda e, cur=cur, prev=prev: e.tensor_tensor(
                out=diff_b[:], in0=cur, in1=prev, op=ALU.subtract))
            for (dst, col) in ((xr_c, 16), (xk_c, 0), (xv_c, 8)):
                p.op("vector", lambda e, dst=dst, c=c, col=col, prev=prev: e.scalar_tensor_tensor(
                    out=dst[c][:], in0=diff_b[:], scalar=c_vec[:, col + c:col + c + 1], in1=prev,
                    op0=ALU.mult, op1=ALU.add))
    v_mixes = S["dve"].n
    v_mixp = S["poolc"].n

    if d_mix is not None:
        p.wait("sync", "dve")
        p.wait("sync", "poolc")
        for i, arr in enumerate((xr_c, xk_c, xv_c)):
            for c in range(NC8):
                p.dma("sync", d_mix[:, (i * 8 + c) * T:(i * 8 + c + 1) * T], arr[c][:], "outs")

    pfree()

    ps_r = palloc("ps_r", [128, 2048], F32)
    ps_v = palloc("ps_v", [128, 2048], F32)

    d_sr = dbg_out("sr", [128, 16384], BF16)
    d_rwkv = dbg_out("rwkv", [128, 16384], BF16)

    wsems = []
    for i in range(4):
        g = nc.semaphore(f"ws{i}")
        h = g.__enter__()
        guards.append(g)
        wsems.append(Sem(h))
    wbsems = []
    for i in range(4):
        g = nc.semaphore(f"wb{i}")
        h = g.__enter__()
        guards.append(g)
        wbsems.append(Sem(h))
    slot_pe_gate = [0, 0, 0, 0]

    def wload(slot, d_blocks, blk, csem):
        p.wait("sync", S["pe"], slot_pe_gate[slot])
        p.wait("sync", S[csem], 16)
        src_ap = d_blocks[blk].rearrange("(kc p) n -> p kc n", p=128)
        dst = wslot[slot].rearrange("p (kc n) -> p kc n", kc=NC8)
        return p.dma("sync", dst, src_ap, wbsems[slot])

    def proj_mm(ps, slot, rhs_c, nk=NC8):
        p.wait("tensor", wsems[slot])
        p.wait("tensor", wbsems[slot])
        for kc in range(nk):
            lhsT = wslot[slot][:, kc * 128:(kc + 1) * 128]
            for n in range(4):
                p.pe(lambda e, lhsT=lhsT, kc=kc, n=n, ps=ps, rhs=rhs_c[kc], nk=nk: e.matmul(
                    ps[:, ts(n, 512)], lhsT, rhs[:, ts(n, 512)],
                    start=(kc == 0), stop=(kc == nk - 1)),
                    inc=(kc == nk - 1 and n == 3))
        slot_pe_gate[slot] = S["pe"].n
        return S["pe"].n

    conv_state = [8]

    def emit_conv(n=1):
        NCCH = 16
        for _ in range(n):
            i = conv_state[0]
            if i >= 2 * NCCH:
                return
            conv_state[0] += 1
            j = i
            half, ci = j % 2, j // 2
            sl = slice(ci * NEXP // NCCH, (ci + 1) * NEXP // NCCH)
            if half == 0:
                p.dma("gpsimd", d_tabb[sl, 0:1024], d_down[sl, :], "tabs")
            else:
                p.dma("gpsimd", d_tabb[sl, 1024:2048], d_up[sl, :], "tabs")

    p.wait("tensor", S["dve"], v_mixes)
    p.wait("tensor", S["poolc"], v_mixp)
    wload(0, d_wrb, 0, "wcr")
    wload(1, d_wrb, 1, "wcr")
    for m in range(NC8):
        v_pe = proj_mm(ps_r, m % 2, xr_c)
        if m + 2 < NC8:
            wload(m % 2, d_wrb, m + 2, "wcr")
        elif m == NC8 - 2:
            wload(0, d_wkb, 0, "wck")
        elif m == NC8 - 1:
            wload(1, d_wkb, 1, "wck")
        emit_conv()
        p.wait("scalar", S["pe"], v_pe)
        v = p.op("scalar", lambda e, m=m: e.activation(out=sr_c[m][:], in_=ps_r[:], func=AF.Sigmoid))
        p.wait("tensor", S["act"], v)

    if d_sr is not None:
        p.wait("sync", "act")
        for c in range(NC8):
            p.dma("sync", d_sr[:, c * T:(c + 1) * T], sr_c[c][:], "outs")

    wload(2, d_wvb, 0, "wcv")
    wload(3, d_wvb, 1, "wcv")
    v_relk, v_relv = [0], [0]
    for m in range(NC8):
        p.wait("tensor", S["act"], v_relk[0])
        v_pek = proj_mm(ps_r, m % 2, xk_c)
        if m + 2 < NC8:
            wload(m % 2, d_wkb, m + 2, "wck")
        elif m == NC8 - 2:
            wload(0, d_wob, 0, "wco")
        elif m == NC8 - 1:
            wload(1, d_wob, 1, "wco")
        p.wait("tensor", S["dve"], v_relv[0])
        v_pev = proj_mm(ps_v, 2 + m % 2, xv_c)
        if m + 2 < NC8:
            wload(2 + m % 2, d_wvb, m + 2, "wcv")
        emit_conv()
        p.wait("vector", S["pe"], v_pek)
        p.op("vector", lambda e: e.reduce_max(stat[:, 0:1], ps_r[:], axis=AX.X))
        p.op("vector", lambda e, m=m: e.tensor_tensor(out=stat[:, 1:2], in0=stat[:, 0:1],
                                                      in1=c_vec[:, 40 + m:41 + m], op=ALU.add))
        p.op("vector", lambda e: e.tensor_scalar(out=stat[:, 2:3], in0=stat[:, 1:2],
                                                 scalar1=-1.0, scalar2=None, op0=ALU.mult))
        v = p.op("vector", lambda e, m=m: e.tensor_tensor(out=stat[:, 3:4], in0=c_vec[:, 32 + m:33 + m],
                                                          in1=stat[:, 1:2], op=ALU.subtract))
        p.wait("scalar", S["dve"], v)
        p.op("scalar", lambda e: e.activation(out=z_t[:], in_=ps_r[:], func=AF.Exp, bias=stat[:, 2:3]))
        v_z2 = p.op("scalar", lambda e: e.activation(out=z2_t[:], in_=ps_r[:], func=AF.Exp, bias=stat[:, 3:4]))
        p.wait("vector", S["act"], v_z2)
        p.wait("vector", S["pe"], v_pev)
        p.op("vector", lambda e: e.tensor_tensor(out=zv_t[:], in0=z_t[:], in1=ps_v[:], op=ALU.mult))
        p.op("vector", lambda e, m=m: e.tensor_tensor_scan(
            out=zv_t[:], data0=c_vec[:, 24 + m:25 + m].broadcast_to([128, T]),
            data1=zv_t[:], initial=0.0, op0=ALU.mult, op1=ALU.add))
        p.op("vector", lambda e, m=m: e.tensor_tensor_scan(
            out=z_t[:], data0=c_vec[:, 24 + m:25 + m].broadcast_to([128, T]),
            data1=z_t[:], initial=0.0, op0=ALU.mult, op1=ALU.add))
        p.op("vector", lambda e: e.tensor_copy(out=den_t[:, 0:1], in_=z2_t[:, 0:1]))
        v_den = p.op("vector", lambda e: e.tensor_tensor(out=den_t[:, 1:T], in0=z_t[:, 0:T - 1],
                                                         in1=z2_t[:, 1:T], op=ALU.add))
        v_euv = p.op("vector", lambda e: e.tensor_tensor(out=z2_t[:], in0=z2_t[:], in1=ps_v[:], op=ALU.mult))
        p.wait("scalar", S["dve"], v_den)
        p.op("scalar", lambda e: e.activation(out=den_t[:], in_=den_t[:], func=AF.Ln))
        v_rden = p.op("scalar", lambda e: e.activation(out=den_t[:], in_=den_t[:], func=AF.Exp, scale=-1.0))
        p.op("vector", lambda e: e.tensor_tensor(out=z2_t[:, 1:T], in0=zv_t[:, 0:T - 1],
                                                 in1=z2_t[:, 1:T], op=ALU.add))
        p.wait("vector", S["act"], v_rden)
        v_y = p.op("vector", lambda e: e.tensor_tensor(out=z2_t[:], in0=z2_t[:], in1=den_t[:], op=ALU.mult))
        p.op("vector", lambda e, m=m: e.tensor_tensor(out=rwkv_c[m][:], in0=sr_c[m][:],
                                                      in1=z2_t[:], op=ALU.mult))
        v_relk[0] = v_z2
        v_relv[0] = v_euv

    if d_rwkv is not None:
        p.wait("sync", "dve")
        for c in range(NC8):
            p.dma("sync", d_rwkv[:, c * T:(c + 1) * T], rwkv_c[c][:], "outs")

    p.wait("tensor", "dve")
    p.wait("tensor", "act")
    p.wait("scalar", "dve")
    for m in range(NC8):
        v_pe = proj_mm(ps_r, m % 2, rwkv_c)
        if m + 2 < NC8:
            wload(m % 2, d_wob, m + 2, "wco")
        elif m == NC8 - 2:
            wload(0, d_wqb, 0, "wcq")
        elif m == NC8 - 1:
            wload(1, d_wqb, 1, "wcq")
        emit_conv()
        zt = z_t if m % 2 == 0 else zv_t
        zsem = "zs0" if m % 2 == 0 else "zs1"
        p.wait("scalar", S["pe"], v_pe)
        p.wait("scalar", S[zsem])
        v = p.op("scalar", lambda e, zt=zt: e.activation(out=zt[:], in_=ps_r[:], func=AF.Copy))
        p.wait("sync", S["act"], v)
        p.dma("sync", d_s[ts(m), :], zt[:], zsem)
        p.wait("tensor", S["act"], v)

    xn1_c = [xn_c[kc][:, 1:T + 1] for kc in range(NC8)]
    for m in range(4):
        v_pe = proj_mm(ps_v, m % 2, xn1_c)
        if m + 2 < 4:
            wload(m % 2, d_wqb, m + 2, "wcq")
        emit_conv(2)
        p.wait("scalar", S["pe"], v_pe)
        v = p.op("scalar", lambda e, m=m: e.activation(out=q_c[m][:], in_=ps_v[:], func=AF.Copy))
        p.wait("tensor", S["act"], v)
    d_q = dbg_out("q", [128, 8192], BF16)
    if d_q is not None:
        p.wait("sync", "act")
        for m in range(4):
            p.dma("sync", d_q[:, m * T:(m + 1) * T], q_c[m][:], "outs")


    if upto == "B":
        return _truncate_finish()

    pfree()

    ps_mlp = palloc("ps_mlp", [128, 2048], F32)
    ps_se = palloc("ps_se", [128, 1024], F32)
    psb2 = palloc("ps_b2", [128, 128], BF16)
    psb3 = palloc("ps_b3", [128, 128], BF16)

    d_dots = dbg_out("dots", [NTT, 128, 32])
    d_expid = dbg_out("expi", [NTT, 128, 32], I32)
    d_probsd = dbg_out("probs", [NTT, 128, 32])
    d_expfm = dbg_out("expfm", [NTT, 1024, 128], BF16)
    d_h1d = dbg_out("h1", [NTB, INTER, TBLK], BF16)

    sx4 = alloc("sx4", [128, 64], F32)
    ixu = alloc("ixu", [128, 64], U32)
    ixf = alloc("ixf", [128, 64], F32)
    itmp = alloc("itmp", [128, 128], I32)
    tops = alloc("tops", [128, 96], F32)
    sel = alloc("sel", [128, 64], F32)
    expi = [alloc(f"expi{i}", [128, 32], I32) for i in range(4)]
    probs2 = alloc("probs2", [128, 128], F32)
    smx = alloc("smx", [128, 16], F32)
    dots = alloc("dots", [128, 32], F32)
    wfin = alloc("wfin", [128, 32], F32)
    sim_s = big0[:, 12288:13312]
    sgb = b0b[:, 26624:28672]
    pdot = [b0b[:, 28672:29696], b0b[:, 29696:30720]]
    keys_t = b0b[:, 30720:31744]
    expstg = b0b[:, 31744:32768]
    diag_g = alloc("diag_g", [128, 512], BF16)
    xtv = xn_tm_t.ap().bitcast(F32)
    cand = xtv[:, 0:256]
    packB = xtv[:, 256:512]


    gslot = ([b0b[:, 8192 + s * 2048:8192 + (s + 1) * 2048] for s in range(8)] +
             [b1b[:, s * 2048:(s + 1) * 2048] for s in range(16)])
    NSLOT = 24
    h1tb = b0b[:, 8192:16384]
    exptb = [b1b[:, 32768 + i * 1024:32768 + (i + 1) * 1024] for i in range(2)]
    xmt = big1[:, 17408:17920]
    wd_b = b1b[:, 0:32768]
    xn_tm = b1b[:, 32768:49152]

    gsems = []
    for i in range(24):
        g = nc.semaphore(f"gs{i}")
        h = g.__enter__()
        guards.append(g)
        gsems.append(Sem(h))
    gslot_pe_gate = [0] * 24
    gslot_cnt = [0] * 24

    emit_conv(40)
    v_tab = S["tabs"].n
    p.wait("gpsimd", "pe")
    p.wait("gpsimd", "act")
    p.dma("gpsimd", keys_t[0:64, :].rearrange("p (g n) -> p g n", g=8),
          d_keys.rearrange("p h n k -> n (p h) k"), "ldk")
    p.dma("gpsimd", keys_t[64:128, :].rearrange("p (g n) -> p g n", g=8),
          d_keys.rearrange("p h n k -> n (p h) k"), "ldk")
    v_keys = S["ldk"].n
    p.dma("gpsimd", d_wgb[0:8], d_wg[0:8], "wcg")
    p.dma("gpsimd", d_wub[0:8], d_wu[0:8], "wcu")
    p.dma("gpsimd", d_wgb[8:NIC], d_wg[8:NIC], "wcg2")
    p.dma("gpsimd", d_wub[8:NIC], d_wu[8:NIC], "wcu2")
    p.dma("gpsimd", d_wdb, d_wd, "wcd")

    p.wait("tensor", "dve")
    p.wait("tensor", "act")
    evac_done = [0, 0]
    for i in range(NTT * NC8):
        tt, c = i // NC8, i % NC8
        pb = psb2 if i % 2 == 0 else psb3
        p.wait("tensor", S["act"], evac_done[i % 2])
        p.pe(lambda e, tt=tt, c=c, pb=pb: e.transpose(
            pb[:], xn_c[c][:, 1 + tt * 128:1 + (tt + 1) * 128], c_identb[:]), inc=True)
        p.wait("scalar", "pe")
        evac_done[i % 2] = p.op("scalar", lambda e, tt=tt, c=c, pb=pb: e.activation(
            out=xn_tm[:, tt * 1024 + c * 128:tt * 1024 + (c + 1) * 128],
            in_=pb[:], func=AF.Copy))
    v_xntm = S["act"].n

    if upto == "CPRE":
        return _truncate_finish()

    p.wait("gpsimd", "pe")
    p.wait("gpsimd", "dve")
    p.wait("gpsimd", "outs")
    p.wait("gpsimd", S["zs0"])
    p.wait("gpsimd", S["zs1"])
    p.wait("tensor", S["ldk"], v_keys)
    g_h1 = nc.semaphore("h1s")
    h1sem = Sem(g_h1.__enter__())
    guards.append(g_h1)

    v_expi_tile = [0, 0, 0, 0]
    pdot_act = [0, 0]
    exp_evac_act = 0
    v_up_prev = 0
    ic_counter = 0
    wd_first_use = True
    silu_act_group = [0] * 16

    topk_pend = [None]
    exp_pend = [None]

    def emit_topk(tt, defer=False):
        if topk_pend[0] is not None:
            for _ in topk_pend[0]:
                pass
            topk_pend[0] = None
        par = tt % 4
        p.wait("tensor", "act")
        for ph in range(8):
            pp, hh = ph // 4, ph % 4
            row = pp * 256 + hh * 64
            qt = q_c[row // 128]
            base = row % 128
            v = p.pe(lambda e, qt=qt, base=base, tt=tt, ph=ph: e.matmul(
                ps_se[:, ts(ph)], qt[base:base + 64, ts(tt)],
                keys_t[base:base + 64, ts(ph)], start=True, stop=True), inc=True)
            p.wait("tensor", S["pe"], v)
        v_pe = S["pe"].n
        p.wait("scalar", S["pe"], v_pe)
        v_evac = p.op("scalar", lambda e: e.activation(out=sim_s[:], in_=ps_se[:], func=AF.Copy))
        gen = topk_stage(tt, v_evac)
        if defer:
            topk_pend[0] = gen
        else:
            for _ in gen:
                pass

    def topk_stage(tt, v_evac):
        par = tt % 4
        p.wait("vector", S["act"], v_evac)
        for ph in range(8):
            p.op("vector", lambda e, ph=ph: e.max(out=sx4[:, ph * 8:(ph + 1) * 8],
                                                  in_=sim_s[:, ts(ph)]))
            p.op("vector", lambda e, ph=ph: e.max_index(out=ixu[:, ph * 8:(ph + 1) * 8],
                                                        in_max=sx4[:, ph * 8:(ph + 1) * 8],
                                                        in_values=sim_s[:, ts(ph)]))
            if ph % 2 == 1:
                yield
        p.op("vector", lambda e: e.tensor_copy(out=ixf[:], in_=ixu[:]))
        yield
        for h in range(4):
            sxv = sx4[:, h * 8:(h + 1) * 8].unsqueeze(2).broadcast_to([128, 8, 8])
            syv = sx4[:, 32 + h * 8:32 + (h + 1) * 8].unsqueeze(1).broadcast_to([128, 8, 8])
            ch = cand[:, h * 64:(h + 1) * 64].rearrange("p (i j) -> p i j", i=8)
            p.op("vector", lambda e, ch=ch, sxv=sxv, syv=syv: e.tensor_tensor(
                out=ch, in0=sxv, in1=syv, op=ALU.add))
        yield
        p.op("vector", lambda e: e.tensor_scalar(out=cand[:], in0=cand[:], scalar1=31.9,
                                                 scalar2=-31.9, op0=ALU.min, op1=ALU.max))
        for h in range(4):
            p.op("vector", lambda e, h=h: e.max(out=tops[:, h * 8:(h + 1) * 8],
                                                in_=cand[:, h * 64:(h + 1) * 64]))
        yield
        p.op("vector", lambda e: e.tensor_scalar(out=cand[:], in0=cand[:], scalar1=32.0,
                                                 scalar2=2048.0, op0=ALU.add, op1=ALU.mult))
        for hh2 in range(2):
            p.op("vector", lambda e, hh2=hh2: e.tensor_copy(out=itmp[:],
                                                            in_=cand[:, hh2 * 128:(hh2 + 1) * 128]))
            p.op("vector", lambda e, hh2=hh2: e.tensor_copy(out=cand[:, hh2 * 128:(hh2 + 1) * 128],
                                                            in_=itmp[:]))
        yield
        ixv = ixf[:, 0:32].unsqueeze(2).broadcast_to([128, 32, 8])
        cB = cand.rearrange("p (hi j) -> p hi j", j=8)
        pB = packB.rearrange("p (hi j) -> p hi j", j=8)
        p.op("vector", lambda e, cB=cB, pB=pB, ixv=ixv: e.scalar_tensor_tensor(
            out=pB, in0=cB, scalar=128.0, in1=ixv, op0=ALU.mult, op1=ALU.add))
        yield
        for h in range(4):
            iyv = ixf[:, 32 + h * 8:32 + (h + 1) * 8].unsqueeze(1).broadcast_to([128, 8, 8])
            ch = cand[:, h * 64:(h + 1) * 64].rearrange("p (i j) -> p i j", i=8)
            p.op("vector", lambda e, ch=ch, iyv=iyv: e.scalar_tensor_tensor(
                out=ch, in0=ch, scalar=128.0, in1=iyv, op0=ALU.mult, op1=ALU.add))
        yield
        for h in range(4):
            p.op("vector", lambda e, h=h: e.max(out=tops[:, 32 + h * 8:32 + (h + 1) * 8],
                                                in_=packB[:, h * 64:(h + 1) * 64]))
            p.op("vector", lambda e, h=h: e.max(out=tops[:, 64 + h * 8:64 + (h + 1) * 8],
                                                in_=cand[:, h * 64:(h + 1) * 64]))
        yield
        p.op("vector", lambda e: e.tensor_copy(out=itmp[:, 0:64], in_=tops[:, 32:96]))
        p.op("vector", lambda e: e.tensor_scalar(out=itmp[:, 64:128], in0=itmp[:, 0:64],
                                                 scalar1=127, scalar2=None, op0=ALU.bitwise_and))
        p.op("vector", lambda e: e.tensor_copy(out=sel[:], in_=itmp[:, 64:128]))
        p.op("vector", lambda e: e.scalar_tensor_tensor(out=sel[:, 0:32],
                                                        in0=sel[:, 0:32], scalar=128.0,
                                                        in1=sel[:, 32:64], op0=ALU.mult, op1=ALU.add))
        yield
        for s8 in range(NSLOT):
            p.wait("vector", gsems[s8], 16 * gslot_cnt[s8])
        vi = p.op("vector", lambda e: e.tensor_copy(out=expi[par][:], in_=sel[:, 0:32]))
        v_expi_tile[par] = vi
        yield
        t8 = tops[:, 0:32].rearrange("p (h r) -> p h r", h=4)
        p.op("vector", lambda e: e.tensor_reduce(out=smx[:, 0:4].rearrange("p (h r) -> p h r", r=1),
                                                 in_=t8, axis=AX.X, op=ALU.max))
        mxv = smx[:, 0:4].rearrange("p (h r) -> p h r", r=1).broadcast_to([128, 4, 8])
        v = p.op("vector", lambda e: e.tensor_tensor(out=t8, in0=t8, in1=mxv, op=ALU.subtract))
        yield
        p.wait("scalar", S["dve"], v)
        v = p.op("scalar", lambda e: e.activation(out=tops[:, 0:32], in_=tops[:, 0:32], func=AF.Exp))
        p.wait("vector", S["act"], v)
        p.op("vector", lambda e: e.tensor_reduce(out=smx[:, 4:8].rearrange("p (h r) -> p h r", r=1),
                                                 in_=t8, axis=AX.X, op=ALU.add))
        p.op("vector", lambda e: e.reciprocal(out=smx[:, 8:12], in_=smx[:, 4:8]))
        rsv = smx[:, 8:12].rearrange("p (h r) -> p h r", r=1).broadcast_to([128, 4, 8])
        pr = probs2[:, par * 32:(par + 1) * 32].rearrange("p (h r) -> p h r", h=4)
        p.op("vector", lambda e, pr=pr, rsv=rsv, t8=t8: e.tensor_tensor(out=pr, in0=t8, in1=rsv, op=ALU.mult))
        if d_expid is not None:
            p.wait("sync", "dve")
            p.dma("sync", d_expid[tt], expi[par][:], "outs")
            p.dma("sync", d_probsd[tt], probs2[:, par * 32:(par + 1) * 32], "outs")

    def emit_experts(tt):
        nonlocal exp_evac_act, v_up_prev
        par = tt % 4
        xnt = xn_tm[:, tt * 1024:(tt + 1) * 1024]
        GS = 2
        NG = 32 // GS
        pend = None

        def finish_group(g4, slots, v_silu):
            nonlocal v_up_prev
            sl4 = slice(g4 * GS, (g4 + 1) * GS)
            p.wait("vector", S["act"], v_silu)
            p.op("vector", lambda e, sl4=sl4, par=par: e.tensor_tensor(
                out=wfin[:, sl4], in0=wfin[:, sl4],
                in1=probs2[:, par * 32 + g4 * GS:par * 32 + (g4 + 1) * GS], op=ALU.mult))
            iv = c_identb.ap().unsqueeze(1).broadcast_to([128, GS, 128])
            wv = wfin[:, sl4].unsqueeze(2).broadcast_to([128, GS, 128])
            dgbase = (g4 % 2) * GS
            dv = diag_g[:, dgbase * 128:(dgbase + GS) * 128].rearrange("p (j c) -> p j c", j=GS)
            p.wait("vector", S["pe"], v_up_prev)
            v_diag = p.op("vector", lambda e, iv=iv, wv=wv, dv=dv: e.tensor_tensor(
                out=dv, in0=iv, in1=wv, op=ALU.mult))
            p.wait("tensor", S["dve"], v_diag)
            for i4, (j, slot) in enumerate(slots):
                for n in range(2):
                    p.pe(lambda e, i4=i4, slot=slot, n=n, j=j, dgbase=dgbase: e.matmul(
                        ps_se[:, ts(n, 512)], diag_g[:, (dgbase + i4) * 128:(dgbase + i4 + 1) * 128],
                        gslot[slot][:, 1024 + n * 512:1024 + (n + 1) * 512],
                        start=(j == 0), stop=(j == 31)),
                        inc=(n == 1))
                gslot_pe_gate[slot] = S["pe"].n
            v_up_prev = S["pe"].n

        fg_count = [0]
        for g4 in range(NG):
            slots = []
            for j4 in range(GS):
                j = g4 * GS + j4
                slot = (tt * 32 + j) % NSLOT
                slots.append((j, slot))
                p.wait("gpsimd", S["pe"], gslot_pe_gate[slot])
                p.wait("gpsimd", S["dve"], v_expi_tile[par])
                p.wait("gpsimd", S["tabs"], v_tab)
                p.dma_indirect_gather(gslot[slot], d_tabb, expi[par][:, j:j + 1], gsems[slot])
                gslot_cnt[slot] += 1
            v_sttd = 0
            for j4x, (j, slot) in enumerate(slots):
                p.wait("vector", gsems[slot], 16 * gslot_cnt[slot])
                p.wait("vector", S["act"], v_xntm)
                if j4x == 0 and g4 % 3 == 0:
                    p.wait("vector", S["act"], silu_act_group[g4])
                    v_sttd = p.opn("vector", lambda e, slot=slot, xnt=xnt, j=j: e.scalar_tensor_tensor(
                        out=gslot[slot][:, 0:1024], in0=gslot[slot][:, 0:1024], scalar=1.0,
                        in1=xnt, op0=ALU.mult, op1=ALU.mult, accum_out=dots[:, j:j + 1]))
                    continue
                pb2 = pdot[j % 2]
                p.wait("vector", S["act"], pdot_act[j % 2])
                vP = p.opn("vector", lambda e, slot=slot, xnt=xnt, pb2=pb2: e.tensor_tensor(
                    out=pb2, in0=gslot[slot][:, 0:1024], in1=xnt, op=ALU.mult))
                p.wait("scalar", S["dve"], vP)
                pdot_act[j % 2] = p.op("scalar", lambda e, pb2=pb2, j=j: e.activation(
                    out=pb2, in_=pb2, func=AF.Copy, accum_out=dots[:, j:j + 1]))
            sl4 = slice(g4 * GS, (g4 + 1) * GS)
            if v_sttd:
                p.wait("scalar", S["dve"], v_sttd)
            v_silu = p.op("scalar", lambda e, sl4=sl4: e.activation(out=wfin[:, sl4],
                                                                    in_=dots[:, sl4], func=AF.Silu))
            silu_act_group[g4] = v_silu
            if pend is not None:
                finish_group(*pend)
            pend = (g4, slots, v_silu)
            gu_quantum()
            for _ in range(2):
                if topk_pend[0] is not None:
                    if next(topk_pend[0], "END") == "END":
                        topk_pend[0] = None
            if exp_pend[0] is not None:
                if next(exp_pend[0], "END") == "END":
                    exp_pend[0] = None
        finish_group(*pend)
        if d_dots is not None:
            p.wait("sync", "dve")
            p.dma("sync", d_dots[tt], dots[:], "outs")
        if exp_pend[0] is not None:
            for _ in exp_pend[0]:
                pass
            exp_pend[0] = None
        p.wait("scalar", S["pe"], v_up_prev)
        p.wait("scalar", "outs")
        v = p.op("scalar", lambda e: e.activation(out=expstg[:], in_=ps_se[:], func=AF.Copy))
        exp_pend[0] = evac_gen(tt, v)

    def evac_gen(tt, v_evac):
        tdone = [v_evac, v_evac]
        for c in range(NC8):
            pb = psb2 if c % 2 == 0 else psb3
            p.wait("tensor", S["act"], tdone[c % 2])
            p.pe(lambda e, c=c, pb=pb: e.transpose(pb[:], expstg[:, ts(c)], c_identb[:]), inc=True)
            p.wait("scalar", "pe")
            tdone[c % 2] = p.op("scalar", lambda e, c=c, pb=pb: e.activation(
                out=expstg[:, ts(c)], in_=pb[:], func=AF.Copy))
            if c % 2 == 1:
                yield
        p.wait("sync", "act")
        p.dma("sync", d_exp[tt].rearrange("(c p) n -> p c n", p=128),
              expstg.rearrange("p (c n) -> p c n", c=NC8), "outs")
        if d_expfm is not None:
            p.dma("sync", d_expfm[tt].rearrange("(c p) n -> p c n", p=128),
                  expstg.rearrange("p (c n) -> p c n", c=NC8), "outs")

    def wload_b(slot, d_blocks, blk):
        p.wait("sync", S["pe"], slot_pe_gate[slot])
        p.wait("sync", S["wcg"], 16)
        p.wait("sync", S["wcu"], 16)
        if blk >= 8:
            p.wait("sync", S["wcg2"], 16)
            p.wait("sync", S["wcu2"], 16)
        src_ap = d_blocks[blk].rearrange("(kc p) n -> p kc n", p=128)
        dst = wslot[slot].rearrange("p (kc n) -> p kc n", kc=NC8)
        return p.dma("sync", dst, src_ap, wbsems[slot])

    NQ = 4
    gu2 = {"ic": 0, "half": 0, "q": 0, "v_pe_g": 0, "v_pe_u": 0,
           "v_silu": 0, "v_h1": 0}

    def gu_end_u(ic):
        p.wait("vector", S["pe"], gu2["v_pe_u"])
        p.wait("vector", S["act"], gu2["v_silu"])
        v_h1 = p.op("vector", lambda e: e.tensor_tensor(out=sgb[:], in0=sgb[:],
                                                        in1=ps_mlp[:], op=ALU.mult))
        gu2["v_h1"] = v_h1
        p.wait("sync", S["dve"], v_h1)
        for tb8 in range(NTB):
            p.dma("sync", d_h1[tb8][ic * 128:(ic + 1) * 128, :],
                  sgb[:, tb8 * TBLK:(tb8 + 1) * TBLK], h1sem)
        if d_h1d is not None:
            for tb8 in range(NTB):
                p.dma("sync", d_h1d[tb8][ic * 128:(ic + 1) * 128, :],
                      sgb[:, tb8 * TBLK:(tb8 + 1) * TBLK], "outs")

    def gu_quantum():
        ic, half, q = gu2["ic"], gu2["half"], gu2["q"]
        if ic >= NIC:
            return
        slot = ic % 2 if half == 0 else 2 + ic % 2
        if q == 0:
            if half == 0:
                if ic > 0:
                    gu_end_u(ic - 1)
                p.wait("tensor", wsems[slot])
                p.wait("tensor", wbsems[slot])
                if ic > 0:
                    p.wait("tensor", S["dve"], gu2["v_h1"])
            else:
                p.wait("scalar", S["pe"], gu2["v_pe_g"])
                p.wait("scalar", S["dve"], gu2["v_h1"])
                p.wait("scalar", h1sem)
                gu2["v_silu"] = p.op("scalar", lambda e: e.activation(
                    out=sgb[:], in_=ps_mlp[:], func=AF.Silu))
                p.wait("tensor", wsems[slot])
                p.wait("tensor", wbsems[slot])
                p.wait("tensor", S["act"], gu2["v_silu"])
        for kc in (2 * q, 2 * q + 1):
            lhsT = wslot[slot][:, kc * 128:(kc + 1) * 128]
            for n in range(4):
                p.pe(lambda e, lhsT=lhsT, kc=kc, n=n: e.matmul(
                    ps_mlp[:, ts(n, 512)], lhsT, xn1_c[kc][:, ts(n, 512)],
                    start=(kc == 0), stop=(kc == NC8 - 1)),
                    inc=(kc == 2 * q + 1 and n == 3))
        if q == NQ - 1:
            slot_pe_gate[slot] = S["pe"].n
            if half == 0:
                gu2["v_pe_g"] = S["pe"].n
                gu2["half"] = 1
            else:
                gu2["v_pe_u"] = S["pe"].n
                gu2["half"] = 0
                gu2["ic"] = ic + 1
            gu2["q"] = 0
        else:
            gu2["q"] = q + 1

    wload_b(0, d_wgb, 0)
    wload_b(1, d_wgb, 1)
    wload_b(2, d_wub, 0)
    wload_b(3, d_wub, 1)
    emit_topk(0)
    emit_topk(1, defer=True)
    for tb in range(NTB):
        ic0 = 4 * tb
        if tb > 0:
            wload_b(ic0 % 2, d_wgb, ic0)
            wload_b(2 + ic0 % 2, d_wub, ic0)
            wload_b((ic0 + 1) % 2, d_wgb, ic0 + 1)
            wload_b(2 + (ic0 + 1) % 2, d_wub, ic0 + 1)
        emit_experts(2 * tb)
        if 2 * tb + 2 < NTT:
            emit_topk(2 * tb + 2, defer=True)
        wload_b((ic0 + 2) % 2, d_wgb, ic0 + 2)
        wload_b(2 + (ic0 + 2) % 2, d_wub, ic0 + 2)
        wload_b((ic0 + 3) % 2, d_wgb, ic0 + 3)
        wload_b(2 + (ic0 + 3) % 2, d_wub, ic0 + 3)
        emit_experts(2 * tb + 1)
        if 2 * tb + 3 < NTT:
            emit_topk(2 * tb + 3, defer=True)
    while gu2["ic"] < NIC:
        gu_quantum()
    gu_end_u(NIC - 1)
    if exp_pend[0] is not None:
        for _ in exp_pend[0]:
            pass
        exp_pend[0] = None
    if topk_pend[0] is not None:
        for _ in topk_pend[0]:
            pass
        topk_pend[0] = None

    if upto == "CI":
        return _truncate_finish()

    p.wait("sync", "pe")
    p.wait("sync", S["wcd"], 16)
    p.dma("sync", wd_b.rearrange("p (ic m) -> p ic m", ic=NIC),
          d_wdb.rearrange("(ic p) m -> p ic m", p=128), "wdl")
    p.wait("tensor", S["pool"])
    p.wait("tensor", S["wdl"], 16)
    p.wait("sync", "act")
    p.wait("sync", "dve")
    p.wait("sync", "pe")
    p.wait("sync", h1sem)
    p.wait("sync", "outs")
    p.wait("sync", S["zs0"])
    p.wait("sync", S["zs1"])
    pfree()
    ps_m2 = [palloc("ps_m2a", [128, 2048], F32), palloc("ps_m2b", [128, 2048], F32)]
    p.wait("gpsimd", h1sem)
    p.wait("gpsimd", "pe")
    p.wait("gpsimd", "dve")
    xnf32 = xn_fm.ap().bitcast(F32)
    mrgT = big1[:, 17920:19968]
    stb = big1[:, 19968:22016]
    xtb = big1[:, 22016:24064]
    h1tb2 = [h1tb, b0b[:, 0:8192]]
    exptb2 = [exptb, [b1b[:, 34816:35840], b1b[:, 48128:49152]]]
    stb2 = [stb, xnf32[:, 0:2048]]
    xtb2 = [xtb, xnf32[:, 2048:4096]]
    mrgT2 = [mrgT, xnf32[:, 4096:6144]]
    v_pe_tb = [0] * NTB
    v_p1_tb = [0] * NTB
    v_t0_tb = [0] * NTB
    v_ld_tb = [0] * NTB

    v_hl_tb = [0] * NTB

    def cii_loads(tb):
        par = tb % 2
        csem = "cl0" if par == 0 else "cl1"
        hsem = "hl0" if par == 0 else "hl1"
        if tb >= 2:
            p.wait("sync", S["pe"], v_pe_tb[tb - 2])
            p.wait("sync", S["dve"], v_p1_tb[tb - 2])
            p.wait("gpsimd", S["pe"], v_pe_tb[tb - 2])
        p.dma("gpsimd", h1tb2[par].rearrange("p (q n) -> p q n", q=NIC),
              d_h1[tb].rearrange("(q p) n -> p q n", p=128), hsem)
        v_hl_tb[tb] = S[hsem].n
        for i in range(2):
            p.dma("sync", exptb2[par][i].rearrange("p (c n) -> p c n", c=NC8),
                  d_exp[2 * tb + i].rearrange("(c p) n -> p c n", p=128), csem)
        p.dma("sync", stb2[par].rearrange("p (m n) -> p m n", m=NC8),
              d_s[:, tb * TBLK:(tb + 1) * TBLK].rearrange("(m p) n -> p m n", p=128), csem)
        p.dma("sync", xtb2[par].rearrange("p (m n) -> p m n", m=NC8),
              d_x[:, tb * TBLK:(tb + 1) * TBLK].rearrange("(m p) n -> p m n", p=128), csem)
        v_ld_tb[tb] = S[csem].n

    cii_loads(0)
    cii_loads(1)
    for tb in range(NTB):
        par = tb % 2
        h1c, expc, stc, xtc, mrgc = (h1tb2[par], exptb2[par], stb2[par],
                                     xtb2[par], mrgT2[par])
        p.wait("tensor", S["cl0" if par == 0 else "cl1"], v_ld_tb[tb])
        p.wait("tensor", S["hl0" if par == 0 else "hl1"], v_hl_tb[tb])
        if tb > 1:
            p.wait("tensor", S["dve"], v_t0_tb[tb - 2])
        psm = ps_m2[par]
        for m in range(NC8):
            for ic in range(NIC):
                p.pe(lambda e, m=m, ic=ic, h1c=h1c, psm=psm: e.matmul(
                    psm[:, m * 256:(m + 1) * 256],
                    wd_b[:, ic * 1024 + m * 128:ic * 1024 + (m + 1) * 128],
                    h1c[:, ic * 256:(ic + 1) * 256],
                    start=(ic == 0), stop=(ic == NIC - 1)), inc=(ic == NIC - 1 and m == NC8 - 1))
        v_pe_tb[tb] = S["pe"].n
        p.wait("vector", S["pe"], v_pe_tb[tb])
        p.wait("vector", S["cl0" if par == 0 else "cl1"], v_ld_tb[tb])
        p.wait("vector", S["outs"])
        mview = mrgc.rearrange("p (m i n) -> p m i n", m=NC8, i=2)
        for i in range(2):
            p.op("vector", lambda e, i=i, mview=mview, expc=expc, psm=psm: e.tensor_tensor(
                out=mview[:, :, i, :],
                in0=psm.ap().rearrange("p (m i n) -> p m i n", m=NC8, i=2)[:, :, i, :],
                in1=expc[i].rearrange("p (c n) -> p c n", c=NC8), op=ALU.add))
        v_t0_tb[tb] = S["dve"].n
        p.wait("scalar", S["dve"], v_t0_tb[tb])
        v_sil = p.op("scalar", lambda e, mrgc=mrgc: e.activation(out=mrgc[:], in_=mrgc[:],
                                                                 func=AF.Silu))
        p.wait("vector", S["act"], v_sil)
        p.op("vector", lambda e, mrgc=mrgc, stc=stc: e.tensor_tensor(
            out=mrgc[:], in0=mrgc[:], in1=stc[:], op=ALU.mult))
        v_p1_tb[tb] = p.op("vector", lambda e, mrgc=mrgc, xtc=xtc: e.tensor_tensor(
            out=mrgc[:], in0=mrgc[:], in1=xtc[:], op=ALU.add))
        p.wait("sync", S["dve"], v_p1_tb[tb])
        p.dma("sync", d_out[:, tb * TBLK:(tb + 1) * TBLK].rearrange("(m p) n -> p m n", p=128),
              mrgc.rearrange("p (m n) -> p m n", m=NC8), "outs")
        if tb + 2 < NTB:
            cii_loads(tb + 2)

    p.wait("sync", "outs")
    p.wait("sync", "pool")
    p.wait("sync", "tabs")
    p.wait("sync", h1sem)
    pfree()
    with nc.Block() as block:
        p.replay(block)

    for g in reversed(guards):
        g.__exit__(None, None, None)
    return nc


def prep_inputs(x_b, tmk, tmv, tmr, td, tf, Wk, Wv, Wr, Wo, Wq):
    x_fm = np.ascontiguousarray(x_b.T).astype(np.float32)
    vecs = np.zeros((128, 64), np.float32)
    for i, v in enumerate((tmk, tmv, tmr, td, tf)):
        vecs[:, i * 8:(i + 1) * 8] = v.reshape(8, 128).T
    def blocks(W):
        WT = W.T.astype(np.float32)
        M = WT.shape[1]
        return np.ascontiguousarray(WT.reshape(WT.shape[0], M // 128, 128).transpose(1, 0, 2))

    return dict(
        x_fm=x_fm, vecs=vecs,
        wrB=blocks(Wr), wkB=blocks(Wk), wvB=blocks(Wv), woB=blocks(Wo), wqB=blocks(Wq),
    )


def prep_inputs_c(keys, down, up, Wg, Wu, Wd):
    def blocks(W):
        WT = W.T.astype(np.float32)
        M = WT.shape[1]
        return np.ascontiguousarray(WT.reshape(WT.shape[0], M // 128, 128).transpose(1, 0, 2))

    return dict(
        keysP=np.ascontiguousarray(np.transpose(keys, (2, 0, 3, 1))).astype(np.float32),
        downE=np.ascontiguousarray(down).astype(np.float32),
        upE=np.ascontiguousarray(up).astype(np.float32),
        wgB=blocks(Wg), wuB=blocks(Wu),
        wdT=np.ascontiguousarray(Wd.T).astype(np.float32),
    )


_NC_CACHE = None


def kernel(**inputs):
    global _NC_CACHE
    from concourse.bass_utils import run_bass_kernel_spmd

    x = np.asarray(inputs["L0_attn_x"])
    shared = prep_inputs_shared(
        np.asarray(inputs["L0_attn_time_mix_k"]),
        np.asarray(inputs["L0_attn_time_mix_v"]),
        np.asarray(inputs["L0_attn_time_mix_r"]),
        np.asarray(inputs["L0_attn_time_decay"]),
        np.asarray(inputs["L0_attn_time_first"]),
        np.asarray(inputs["L0_attn_Wk"]), np.asarray(inputs["L0_attn_Wv"]),
        np.asarray(inputs["L0_attn_Wr"]), np.asarray(inputs["L0_attn_Wo"]),
        np.asarray(inputs["L0_ffn_Wq"]), np.asarray(inputs["L0_ffn_keys"]),
        np.asarray(inputs["L0_ffn_down_embed"]), np.asarray(inputs["L0_ffn_up_embed"]),
        np.asarray(inputs["L0_ffn_Wg"]), np.asarray(inputs["L0_ffn_Wu"]),
        np.asarray(inputs["L0_ffn_Wd"]))

    in_maps = []
    for b in range(8):
        m = dict(shared)
        m["x_fm"] = np.ascontiguousarray(x[b].T).astype(np.float32)
        in_maps.append(m)

    if _NC_CACHE is None:
        _NC_CACHE = build()
    res = run_bass_kernel_spmd(_NC_CACHE, in_maps, core_ids=list(range(8)))
    out = np.stack([np.asarray(res.results[b]["out_fm"]).T for b in range(8)], 0)
    return np.ascontiguousarray(out).astype(np.float32)


def prep_inputs_shared(tmk, tmv, tmr, td, tf, Wk, Wv, Wr, Wo, Wq, keys, down, up, Wg, Wu, Wd):
    vecs = np.zeros((128, 64), np.float32)
    for i, v in enumerate((tmk, tmv, tmr, td, tf)):
        vecs[:, i * 8:(i + 1) * 8] = v.reshape(8, 128).T

    def blocks(W):
        WT = W.T.astype(np.float32)
        M = WT.shape[1]
        return np.ascontiguousarray(WT.reshape(WT.shape[0], M // 128, 128).transpose(1, 0, 2))

    return dict(
        vecs=vecs,
        wrB=blocks(Wr), wkB=blocks(Wk), wvB=blocks(Wv), woB=blocks(Wo), wqB=blocks(Wq),
        wgB=blocks(Wg), wuB=blocks(Wu),
        wdT=np.ascontiguousarray(Wd.T).astype(np.float32),
        keysP=np.ascontiguousarray(np.transpose(keys, (2, 0, 3, 1))).astype(np.float32),
        downE=np.ascontiguousarray(down).astype(np.float32),
        upE=np.ascontiguousarray(up).astype(np.float32),
    )
